# version 19
# baseline (speedup 1.0000x reference)
import numpy as np
import ml_dtypes
import concourse.bass as bass
import concourse.mybir as mybir
from concourse.bass_utils import run_bass_kernel_spmd
from concourse.bass_core_rust import IndirectOffsetOnAxis
from contextlib import ExitStack

F32 = mybir.dt.float32
BF16 = mybir.dt.bfloat16
FP16 = mybir.dt.float16
I32 = mybir.dt.int32
AF = mybir.ActivationFunctionType
ALU = mybir.AluOpType
AX = mybir.AxisListType

T, D, NH, DH, E, FF = 2048, 1024, 16, 64, 8, 4096
CAP = 513
S = CAP + 1
SH = 257
SP = 544
NST = 5
TT = T // 128
DC = D // 128
HT = FF // 128
NQC = 4
HWIN = 256
DWIN = 256
EXP_SCALE = 0.125
EXP_BIAS = -20.0
LN_EPS = 1e-5
NBIS = 13
KB_ = 1024
U = DH + 1


class KB:
    def __init__(self, nc, ctx):
        self.nc, self.ctx, self.sems, self.cnt = nc, ctx, {}, {}

    def sem(self, name):
        if name not in self.sems:
            self.sems[name] = self.ctx.enter_context(self.nc.semaphore(name))
            self.cnt[name] = 0
        return self.sems[name]

    def inc(self, inst, name, by=1):
        if name is None:
            return None
        inst.then_inc(self.sem(name), by)
        self.cnt[name] += by
        return self.cnt[name]

    def val(self, name):
        self.sem(name)
        return self.cnt[name]


def build(dbg_names=(), stop_after=99):
    nc = bass.Bass("TRN2", target_bir_lowering=False, debug=False,
                   dynamic_dma_scratch_size=16384)

    x_in = nc.dram_tensor("x", [T, D], F32, kind="ExternalInput").ap()
    w_in = {k: nc.dram_tensor(k, [D, D], F32, kind="ExternalInput").ap()
            for k in ("wk", "wq", "wv", "wp")}
    rw_in = nc.dram_tensor("rw", [E, D], F32, kind="ExternalInput").ap()
    w1_in = nc.dram_tensor("w1", [E, D, FF], F32, kind="ExternalInput").ap()
    b1_in = nc.dram_tensor("b1", [E, FF], F32, kind="ExternalInput").ap()
    w2_in = nc.dram_tensor("w2", [E, FF, D], F32, kind="ExternalInput").ap()
    b2_in = nc.dram_tensor("b2", [E, D], F32, kind="ExternalInput").ap()
    out_t = nc.dram_tensor("out", [T, D], F32, kind="ExternalOutput").ap()

    id_bf_in = nc.dram_tensor("id_bf", [128, 128], BF16, kind="ExternalInput").ap()
    id_f32_in = nc.dram_tensor("id_f32", [128, 128], F32, kind="ExternalInput").ap()
    cmask_in = nc.dram_tensor("cmask", [4, 128, 512], BF16, kind="ExternalInput").ap()
    iota_in = nc.dram_tensor("iota_s", [128, S], FP16, kind="ExternalInput").ap()
    jcol_in = nc.dram_tensor("jcol", [128, 1], F32, kind="ExternalInput").ap()
    brep_in = nc.dram_tensor("brep", [E, 128], F32, kind="ExternalInput").ap()
    bseg_in = nc.dram_tensor("bseg", [128, E], F32, kind="ExternalInput").ap()
    ones_in = nc.dram_tensor("ones_f", [128, 128], F32, kind="ExternalInput").ap()
    onesb_in = nc.dram_tensor("ones_b", [128, 128], BF16, kind="ExternalInput").ap()
    pccol_in = nc.dram_tensor("pccol", [128, 2 + TT], BF16, kind="ExternalInput").ap()

    xbf_d = nc.dram_tensor("xbf_d", [T, D], BF16, kind="Internal").ap()
    wbf_d = nc.dram_tensor("wbf_d", [4, D, D], BF16, kind="Internal").ap()
    resid2_d = nc.dram_tensor("resid2_d", [T, D], F32, kind="Internal").ap()
    w1bf_d = nc.dram_tensor("w1bf_d", [E, D, FF], BF16, kind="Internal").ap()
    w2bf_d = nc.dram_tensor("w2bf_d", [E, FF, D], BF16, kind="Internal").ap()

    dbg_handles = {}
    ctx = ExitStack()
    with ctx:
        sb = nc.alloc_sbuf_tensor
        IDB = sb("IDB", [128, 128], BF16)
        IDF = sb("IDF", [128, 128], F32)
        ONESF = sb("ONESF", [128, 128], F32)
        ONESB = sb("ONESB", [128, 128], BF16)
        JCOL = sb("JCOL", [128, 1], F32)
        BREP = sb("BREP", [E, 128], F32)
        BSEG = sb("BSEG", [128, E], F32)
        PCCOL = sb("PCCOL", [128, 2 + TT], BF16)
        RWT = sb("RWT", [128, DC * E], BF16)
        RWPT = sb("RWPT", [128, DC * E], BF16)
        SRWB = sb("SRWB", [128, E], F32)
        xraw_sb = sb("XRAW", [128, TT * E], F32)
        LOW8 = sb("LOW8", [E, 2], F32)
        BN6 = sb("BN6", [128, 12], F32)
        MV2 = sb("MV2", [128, 2], F32)
        SDV = sb("SDV", [128, 1], F32)
        RSD2 = sb("RSD2", [128, 1], F32)
        LG8 = sb("LG8", [128, E], F32)
        MS8 = sb("MS8", [128, E], F32)
        NMX = sb("NMX", [128, 1], F32)
        SMS = sb("SMS", [128, 1], F32)
        SMR = sb("SMR", [128, 1], F32)
        CNT1 = sb("CNT1", [128, 1], F32)
        FLG1 = sb("FLG1", [128, 1], F32)
        LWR = sb("LWR", [128, 2], F32)
        TAU = sb("TAU", [128, 1], F32)
        TAU2 = sb("TAU2", [128, 1], F32)
        JS8 = sb("JS8", [E, 1], F32)
        JW8 = sb("JW8", [E, 1], F32)
        TMP8 = sb("TMP8", [128, E], F32)
        CONSTT = sb("CONSTT", [128, 2], F32)
        CNTD = sb("CNTD", [128, NBIS], F32)
        TAUD = sb("TAUD", [128, NBIS], F32)
        JSD = sb("JSD", [E, NBIS], F32)

        TOP = nc.sbuf_top
        szA, szB, szC, szD, szE, szF, szG = (33 * KB_ + 512, 32 * KB_, 32 * KB_,
                                             26 * KB_, 32 * KB_, 32 * KB_, 16 * KB_)
        base = TOP - (szA + szB + szC + szD + szE + szF + szG)
        offA = base
        offB = offA + szA
        offC = offB + szB
        offD = offC + szC
        offE = offD + szD
        offF = offE + szE
        offG = offF + szF

        at = nc.alloc_sbuf_tensor_at

        V = at("V", [128, TT * NH * U], BF16, offset=offA)
        R2T = [at(f"R2T{i}", [128, D], F32, offset=offA + i * 4 * KB_) for i in range(2)]
        R2T += [at(f"R2T{2 + i}", [128, D], F32, offset=offD + i * 4 * KB_)
                for i in range(2)]
        OUTS2 = [at(f"OUTS{2 + i}", [128, D], F32, offset=offD + (2 + i) * 4 * KB_)
                 for i in range(2)]
        OUTS = [at(f"OUTS{i}", [128, D], F32, offset=offA + (2 + i) * 4 * KB_)
                for i in range(2)]
        OUTS += OUTS2
        IOTA = at("IOTA", [128, S], FP16, offset=offA + 16 * KB_)
        R_sb = at("R_sb", [128, TT * E], F32, offset=offA + 18 * KB_)
        MSTAT = at("MSTAT", [128, TT], F32, offset=offA + 19 * KB_)
        RSTAT = at("RSTAT", [128, TT], F32, offset=offA + 19 * KB_ + 128)
        GMT = at("GMT", [128, TT * E], F32, offset=offA + 20 * KB_)
        CTT = at("CTT", [128, TT * E], F32, offset=offA + 21 * KB_)
        QT = at("QT", [128, DC * T], BF16, offset=offB)
        RWB16 = at("RWB16", [E, D], BF16, offset=offB)
        RT = at("RT", [E, T], F32, offset=offB)
        REPS = at("REPS", [128, T], F32, offset=offB + 8 * KB_)
        R1T = [at(f"R1T{i}", [128, D], F32, offset=offB + 16 * KB_ + i * 4 * KB_)
               for i in range(2)]
        XRES = [at(f"XRES{i}", [128, D], F32, offset=offB + 24 * KB_ + i * 4 * KB_)
                for i in range(2)]
        HTB = at("HTB", [128, HT * SP], BF16, offset=offB)
        KT = at("KT", [128, DC * T], BF16, offset=offC)
        BSCR = at("BSCR", [128, T], F32, offset=offC + 2 * KB_)
        MSK = at("MSK", [E, T], F32, offset=offC + 10 * KB_)
        CS = at("CS", [E, T], F32, offset=offC + 18 * KB_)
        B1F = at("B1F", [E, FF], F32, offset=offC + 2 * KB_)
        W2S = [at("W2S0", [128, HT * DWIN], BF16, offset=offC + 2 * KB_),
               None]
        B2F = at("B2F", [E, D], F32, offset=offC + 18 * KB_)
        GME = at("GME", [E, T], BF16, offset=offA + 22 * KB_)
        B1T = at("B1T", [128, HT * E], F32, offset=offA + 26 * KB_)
        B2B = at("B2B", [E, D], BF16, offset=offA + 27 * KB_)
        RWF = at("RWF", [E, D], F32, offset=offD)
        WPN = at("WPN", [128, DC * D], BF16, offset=offD + 4 * KB_)
        PPROB = at("PPROB", [128, 8 * 512], BF16, offset=offD + 4 * KB_)
        RCP = at("RCP", [65, 512], F32, offset=offD + 12 * KB_)
        BCS = at("BCS", [64, 512], F32, offset=offD + 14 * KB_)
        CMASK = at("CMASK", [128, 4 * 512], BF16, offset=offD + 16 * KB_)
        YB = at("YB", [128, NST * (D // 2)], F32, offset=offD)
        YB2 = at("YB2", [128, NST * (D // 2)], F32, offset=offA)
        YBH = [YB, YB2]
        W1S = [at("W1S0", [128, DC * HWIN], BF16, offset=offA + 29 * KB_),
               None]
        W2S[1] = at("W2S1", [128, HT * DWIN], BF16, offset=offD + 10 * KB_)
        XT = at("XT", [128, DC * T], BF16, offset=offE)
        OT = at("OT", [128, DC * T], BF16, offset=offE)
        PT = at("PT", [128, TT * S], BF16, offset=offE)
        XGT = at("XGT", [128, DC * S], BF16, offset=offE + 17 * KB_)
        W1S[1] = at("W1S1", [128, DC * HWIN], BF16, offset=offE + 26 * KB_)
        WSTG = at("WSTG", [128, 4 * D], BF16, offset=offF)
        WT0 = at("WT0", [128, DC * D], BF16, offset=offF + 8 * KB_)
        XLN = at("XLN", [128, TT * D], BF16, offset=offF)
        WT1 = at("WT1", [128, DC * D], BF16, offset=offG)
        GHL = at("GHL", [128, TT * E * 4], BF16, offset=offG)
        R4 = at("R4", [4, NST * 128], F32, offset=offG + 2 * KB_)
        SLS = at("SLS", [128, 4], F32, offset=offG + 5 * KB_)
        GS = at("GS", [128, NST], F32, offset=offG + 5 * KB_ + 128)
        FIDX = at("FIDX", [128, 1], F32, offset=offG + 6 * KB_)
        IDXC = at("IDXC", [128, NST], I32, offset=offG + 6 * KB_ + 128)
        GS2 = at("GS2", [128, NST], F32, offset=offG + 7 * KB_)
        IDXC2 = at("IDXC2", [128, NST], I32, offset=offG + 7 * KB_ + 128)
        GSB = [GS, GS2]
        IDXCB = [IDXC, IDXC2]
        RWPE = at("RWPE", [E, D], BF16, offset=offD + 20 * KB_)

        blk = ctx.enter_context(nc.Block())
        kb = KB(nc, ctx)

        @blk.sync
        def _(e):
            for dst, src in [(IDB, id_bf_in), (IDF, id_f32_in), (ONESF, ones_in),
                             (ONESB, onesb_in), (JCOL, jcol_in), (BREP, brep_in),
                             (BSEG, bseg_in), (PCCOL, pccol_in), (RWF, rw_in)]:
                kb.inc(e.dma_start(out=dst[:], in_=src[:]), "c_l", 16)

        NCONST = kb.val("c_l")

        XCT = [at(f"XCT{i}", [128, D], BF16, offset=offD + 4 * KB_ + i * 2 * KB_)
               for i in range(4)]
        px = ExitStack()
        TRXB = [px.enter_context(nc.psum_tensor(f"trxb{i}", [128, 128], BF16))
                for i in range(2)]

        def emit_xc(tts):
            @blk.gpsimd
            def _(e):
                for tt in tts:
                    if tt >= 4:
                        e.wait_ge(kb.sem("xt_mm"), (tt - 3) * DC)
                    kb.inc(e.dma_start(out=XCT[tt % 4][:],
                                       in_=x_in[tt * 128:(tt + 1) * 128, :]),
                           f"xc_l{tt % 2}", 16)

        def emit_wcast(names):
            @blk.gpsimd
            def _(e):
                for i, wname in names:
                    for h in range(2):
                        kb.inc(e.dma_start(out=wbf_d[i][h * 512:(h + 1) * 512],
                                           in_=w_in[wname][h * 512:(h + 1) * 512]),
                               "wbf_w", 16)

        emit_xc(range(0, 8))
        emit_wcast([(0, "wk"), (1, "wq")])
        emit_xc(range(8, TT))
        emit_wcast([(2, "wv"), (3, "wp")])

        for i in range(4):
            def emit_wt(i=i):
                @blk.sync
                def _(e):
                    e.wait_ge(kb.sem("wbf_w"), 32 * (i + 1))
                    if i == 2:
                        e.wait_ge(kb.sem("kt_mm"), 32)
                    if i == 3:
                        e.wait_ge(kb.sem("qt_mm"), 32)
                    slot = WT0 if i % 2 == 0 else WT1
                    for ct in range(DC):
                        kb.inc(e.dma_start(out=slot[:, ct * D:(ct + 1) * D],
                                           in_=wbf_d[i][:, ct * 128:(ct + 1) * 128],
                                           transpose=True), f"wt_l{i % 2}", 16)
            emit_wt()

        @blk.tensor
        def _(e):
            e.wait_ge(kb.sem("c_l"), NCONST)
            n = 0
            for tt in range(TT):
                e.wait_ge(kb.sem(f"xc_l{tt % 2}"), 16 * (tt // 2 + 1))
                for ct in range(DC):
                    if n >= 2:
                        e.wait_ge(kb.sem("xt_ev"), n - 1)
                    kb.inc(e.transpose(TRXB[n % 2][:],
                                       XCT[tt % 4][:, ct * 128:(ct + 1) * 128],
                                       IDB[:]), "xt_mm")
                    n += 1

        @blk.vector
        def _(e):
            n = 0
            for tt in range(TT):
                for ct in range(DC):
                    e.wait_ge(kb.sem("xt_mm"), n + 1)
                    kb.inc(e.tensor_copy(XT[:, ct * T + tt * 128: ct * T + tt * 128 + 128],
                                         TRXB[n % 2][:]), "xt_ev")
                    n += 1


        p1 = ExitStack()
        with p1:
            TRP = p1.enter_context(nc.psum_tensor("trp", [128, 128], BF16))
            TRF = p1.enter_context(nc.psum_tensor("trf", [128, 128], F32))

            @blk.vector
            def _(e):
                kb.inc(e.memset(CONSTT[:, 0:1], EXP_BIAS), "cst_z")
                kb.inc(e.memset(CONSTT[:, 1:2], LN_EPS), "cst_z")
                e.wait_ge(kb.sem("c_l"), NCONST)
                e.drain()
                kb.inc(e.tensor_reduce(SRWB[0:E, 0:1], RWF[:], AX.X, ALU.add), "srw_r")
                e.drain()
                kb.inc(e.tensor_copy(RWB16[:], RWF[:]), "rwb_c")

            @blk.tensor
            def _(e):
                e.wait_ge(kb.sem("rwb_c"), 1)
                for ct in range(DC):
                    if ct >= 1:
                        e.wait_ge(kb.sem("rwt_ev"), ct)
                    kb.inc(e.transpose(TRP[:, 0:E], RWB16[:, ct * 128:(ct + 1) * 128],
                                       IDB[0:E, 0:E]), "rwt_mm")

            @blk.vector
            def _(e):
                for ct in range(DC):
                    e.wait_ge(kb.sem("rwt_mm"), ct + 1)
                    kb.inc(e.tensor_copy(RWT[:, ct * E:(ct + 1) * E], TRP[:, 0:E]), "rwt_ev")

            @blk.tensor
            def _(e):
                e.wait_ge(kb.sem("srw_r"), 1)
                kb.inc(e.transpose(TRF[0:1, 0:E], SRWB[0:E, 0:1], IDF[0:E, 0:E]), "srwt_mm")

            @blk.vector
            def _(e):
                e.wait_ge(kb.sem("srwt_mm"), 1)
                kb.inc(e.tensor_copy(SRWB[0:1, 0:E], TRF[0:1, 0:E]), "srwt_ev")

            @blk.tensor
            def _(e):
                e.wait_ge(kb.sem("srwt_ev"), 1)
                kb.inc(e.matmul(TRF[:, 0:E], ONESF[0:1, :], SRWB[0:1, 0:E],
                                start=True, stop=True), "srwb_mm")

            @blk.vector
            def _(e):
                e.wait_ge(kb.sem("srwb_mm"), 1)
                kb.inc(e.tensor_copy(SRWB[:, 0:E], TRF[:, 0:E]), "srwb_ev")

            @blk.gpsimd
            def _(e):
                e.wait_ge(kb.sem("wbf_w"), 128)
                e.wait_ge(kb.sem("xt_mm"), TT * DC)
                kb.inc(e.dma_start(out=WPN[:].rearrange("p (c f) -> p c f", f=D),
                                   in_=wbf_d[3].rearrange("(c p) d -> p c d", p=128)),
                       "wpn_l", 16)

            RWPP = p1.enter_context(nc.psum_tensor("rwpp", [E, 512], F32))

            @blk.tensor
            def _(e):
                e.wait_ge(kb.sem("wpn_l"), 16)
                e.wait_ge(kb.sem("rwt_ev"), DC)
                for oc in range(2):
                    if oc == 1:
                        e.wait_ge(kb.sem("rwp_ev"), 1)
                    for pc in range(DC):
                        mm = e.matmul(RWPP[:], RWT[:, pc * E:(pc + 1) * E],
                                      WPN[:, pc * D + oc * 512: pc * D + oc * 512 + 512],
                                      start=(pc == 0), stop=(pc == DC - 1))
                    kb.inc(mm, "rwp_mm")

            @blk.vector
            def _(e):
                for oc in range(2):
                    e.wait_ge(kb.sem("rwp_mm"), oc + 1)
                    kb.inc(e.tensor_copy(RWPE[:, oc * 512:(oc + 1) * 512], RWPP[:]),
                           "rwp_ev")

            @blk.tensor
            def _(e):
                e.wait_ge(kb.sem("rwp_ev"), 2)
                for ot in range(DC):
                    if ot >= 1:
                        e.wait_ge(kb.sem("rwpt_ev"), ot)
                    kb.inc(e.transpose(TRP[:, 0:E], RWPE[:, ot * 128:(ot + 1) * 128],
                                       IDB[0:E, 0:E]), "rwpt_mm")

            @blk.vector
            def _(e):
                for ot in range(DC):
                    e.wait_ge(kb.sem("rwpt_mm"), ot + 1)
                    kb.inc(e.tensor_copy(RWPT[:, ot * E:(ot + 1) * E], TRP[:, 0:E]),
                           "rwpt_ev")

        px.close()

        if stop_after < 1:
            return _epilog(nc, kb, blk, out_t, locals(), dbg_names, dbg_handles)

        p2 = ExitStack()
        with p2:
            PRJ = [p2.enter_context(nc.psum_tensor(f"prj{i}", [128, 512], F32))
                   for i in range(4)]

            def proj_omajor(tag, slot, dst, wt_sem, wt_val):
                @blk.tensor
                def _(e):
                    e.wait_ge(kb.sem(wt_sem), wt_val)
                    e.wait_ge(kb.sem("xt_ev"), TT * DC)
                    if tag == "kt":
                        e.wait_ge(kb.sem("rwpt_ev"), DC)
                    base = 0 if tag == "kt" else 2
                    n = 0
                    for ot in range(DC):
                        for tc in range(NQC):
                            ps = PRJ[base + n % 2]
                            if n >= 2:
                                e.wait_ge(kb.sem(f"{tag}_ev"), n - 1)
                            for ct in range(DC):
                                mm = e.matmul(
                                    ps[:], slot[:, ct * D + ot * 128: ct * D + ot * 128 + 128],
                                    XT[:, ct * T + tc * 512: ct * T + tc * 512 + 512],
                                    start=(ct == 0), stop=(ct == DC - 1))
                            kb.inc(mm, f"{tag}_mm")
                            n += 1

                @blk.vector
                def _(e):
                    base = 0 if tag == "kt" else 2
                    n = 0
                    for ot in range(DC):
                        for tc in range(NQC):
                            e.wait_ge(kb.sem(f"{tag}_mm"), n + 1)
                            kb.inc(e.tensor_scalar_mul(
                                dst[:, ot * T + tc * 512: ot * T + tc * 512 + 512],
                                PRJ[base + n % 2][:], 1.0 / 32.0), f"{tag}_ev")
                            n += 1

            proj_omajor("kt", WT0, KT, "wt_l0", 16 * DC)
            proj_omajor("qt", WT1, QT, "wt_l1", 16 * DC)

            @blk.vector
            def _(e):
                kb.inc(e.memset(V[:], 0.0), "v_init")
                e.drain()
                kb.inc(e.memset(
                    V[:].rearrange("p (t h u) -> p t h u", h=NH, u=U)[:, :, :, DH:U],
                    1.0), "v_init")

            @blk.tensor
            def _(e):
                e.wait_ge(kb.sem("wt_l0"), 16 * DC * 2)
                n = 0
                for tt in range(TT):
                    for oc in range(2):
                        ps = PRJ[n % 2]
                        if n < 2:
                            e.wait_ge(kb.sem("kt_ev"), 31 + n)
                        else:
                            e.wait_ge(kb.sem("v_ev"), n - 1)
                        for ct in range(DC):
                            mm = e.matmul(ps[:], XT[:, ct * T + tt * 128: ct * T + tt * 128 + 128],
                                          WT0[:, ct * D + oc * 512: ct * D + oc * 512 + 512],
                                          start=(ct == 0), stop=(ct == DC - 1))
                        kb.inc(mm, "v_mm")
                        n += 1

            @blk.vector
            def _(e):
                e.wait_ge(kb.sem("v_init"), 2)
                e.drain()
                n = 0
                for tt in range(TT):
                    for oc in range(2):
                        e.wait_ge(kb.sem("v_mm"), n + 1)
                        dstv = V[:].rearrange("p (t h u) -> p t h u", h=NH, u=U)[
                            :, tt, oc * 8:(oc + 1) * 8, 0:DH]
                        kb.inc(e.tensor_scalar_mul(
                            dstv, PRJ[n % 2][:].rearrange("p (h u) -> p h u", u=DH),
                            1.0 / 32.0), "v_ev")
                        n += 1

            XRWP = [p2.enter_context(nc.psum_tensor(f"xrw{i}", [128, E], F32))
                    for i in range(2)]

            @blk.tensor
            def _(e):
                e.wait_ge(kb.sem("rwt_ev"), DC)
                for tt in range(TT):
                    if tt >= 2:
                        e.wait_ge(kb.sem("xraw_ev"), tt - 1)
                    for ct in range(DC):
                        mm = e.matmul(XRWP[tt % 2][:],
                                      XT[:, ct * T + tt * 128: ct * T + tt * 128 + 128],
                                      RWT[:, ct * E:(ct + 1) * E],
                                      start=(ct == 0), stop=(ct == DC - 1))
                    kb.inc(mm, "xraw_mm")

            @blk.vector
            def _(e):
                for tt in range(TT):
                    e.wait_ge(kb.sem("xraw_mm"), tt + 1)
                    kb.inc(e.tensor_copy(xraw_sb[:, tt * E:(tt + 1) * E], XRWP[tt % 2][:]),
                           "xraw_ev")

        if stop_after < 2:
            return _epilog(nc, kb, blk, out_t, locals(), dbg_names, dbg_handles)

        rounds = [(h, qi) for h in range(NH) for qi in range(NQC)]
        blocks = []
        for r, (h, qi) in enumerate(rounds):
            for ki in range(4 * qi + 4):
                blocks.append((r, h, qi, ki, ki >= 4 * qi))
        NB = len(blocks)
        pmx_of = {}
        pmn = 0
        for n, (r, h, qi, ki, dg) in enumerate(blocks):
            if dg:
                pmn += 1
                pmx_of[n] = pmn
        def slot_of(n):
            return 2 * ((n // 2) % 4) + (n % 2)

        slot_prev_av = [None] * NB
        last_use = {}
        for n in range(NB):
            s = slot_of(n)
            slot_prev_av[n] = last_use.get(s)
            last_use[s] = n

        p3 = ExitStack()
        with p3:
            SCP2 = [p3.enter_context(nc.psum_tensor(f"scp{i}", [128, 1024], F32))
                    for i in range(2)]
            AVP = [p3.enter_context(nc.psum_tensor(f"avp{i}", [65, 512], F32))
                   for i in range(2)]
            BCP = p3.enter_context(nc.psum_tensor("bcp", [64, 512], F32))

            @blk.sync
            def _(e):
                e.wait_ge(kb.sem("rwp_mm"), 2)
                kb.inc(e.dma_start(out=CMASK[:].rearrange("p (c f) -> p c f", f=512),
                                   in_=cmask_in.rearrange("c p f -> p c f")), "cm_l", 16)

            @blk.tensor
            def _(e):
                e.wait_ge(kb.sem("qt_ev"), 32)
                e.wait_ge(kb.sem("kt_ev"), 32)
                e.wait_ge(kb.sem("v_ev"), 32)
                e.wait_ge(kb.sem("xraw_ev"), TT)
                e.wait_ge(kb.sem("rwpt_ev"), DC)
                NPAIR = NB // 2

                def emit_sc_pair(pr):
                    if pr >= 2:
                        e.wait_ge(kb.sem("pex"), pr - 1)

                    for n in (2 * pr, 2 * pr + 1):
                        r, h, qi, ki, dg = blocks[n]
                        ot, rb = h // 2, 64 * (h % 2)
                        kb.inc(e.matmul(
                            SCP2[pr % 2][:, (n % 2) * 512:(n % 2) * 512 + 512],
                            KT[rb:rb + 64, ot * T + ki * 128: ot * T + ki * 128 + 128],
                            QT[rb:rb + 64, ot * T + qi * 512: ot * T + qi * 512 + 512],
                            start=True, stop=True), "sc_mm")

                ended_rounds = []

                def emit_av_pair(pr):
                    for n in (2 * pr, 2 * pr + 1):
                        r, h, qi, ki, dg = blocks[n]
                        kmax = 4 * qi + 4
                        if ki == 0 and r >= 2:
                            e.wait_ge(kb.sem("ot_ev"), r - 1)
                        if dg:
                            e.wait_ge(kb.sem("pmx"), pmx_of[n])
                        else:
                            e.wait_ge(kb.sem("pex"), pr + 1)
                        kb.inc(e.matmul(
                            AVP[r % 2][:],
                            V[:, ki * NH * U + h * U: ki * NH * U + h * U + U],
                            PPROB[:, slot_of(n) * 512: slot_of(n) * 512 + 512],
                            start=(ki == 0), stop=(ki == kmax - 1)), "av_mm")
                        if ki == kmax - 1:
                            ended_rounds.append(r)

                def emit_bc(rounds_list):
                    for r in rounds_list:
                        e.wait_ge(kb.sem("rc_ev"), r + 1)
                        kb.inc(e.matmul(BCP[:], ONESF[64:65, 0:64], RCP[64:65, :],
                                        start=True, stop=True), "bc_mm")

                for p in range(4):
                    emit_sc_pair(p)
                pending_bc = []
                for k in range(2, NPAIR // 2):
                    emit_sc_pair(2 * k)
                    emit_sc_pair(2 * k + 1)
                    ended_rounds.clear()
                    emit_av_pair(2 * k - 4)
                    emit_av_pair(2 * k - 3)
                    emit_bc(pending_bc)
                    pending_bc = list(ended_rounds)
                for p in range(NPAIR - 4, NPAIR):
                    ended_rounds.clear()
                    emit_av_pair(p)
                    emit_bc(pending_bc)
                    pending_bc = list(ended_rounds)
                emit_bc(pending_bc)

            conv_chunks = []
            for ex in range(E):
                for c in range(8):
                    conv_chunks.append((w1bf_d[ex][c * 128:(c + 1) * 128],
                                        w1_in[ex][c * 128:(c + 1) * 128]))
                for c in range(8):
                    conv_chunks.append((w2bf_d[ex][c * 512:(c + 1) * 512],
                                        w2_in[ex][c * 512:(c + 1) * 512]))
            NCH = len(conv_chunks)

            @blk.gpsimd
            def _(e):
                e.wait_ge(kb.sem("cm_l"), 16)
                ci = 0
                for n, (r, h, qi, ki, dg) in enumerate(blocks):
                    if dg:
                        c = ki - 4 * qi
                        e.wait_ge(kb.sem("pex"), n // 2 + 1)
                        sl = PPROB[:, slot_of(n) * 512: slot_of(n) * 512 + 512]
                        kb.inc(e.tensor_mul(sl, sl, CMASK[:, c * 512:(c + 1) * 512]),
                               "pmx")
                    while ci < NCH and ci * (NB * 7 // 8) <= n * NCH:
                        o_, i_ = conv_chunks[ci]
                        kb.inc(e.dma_start(out=o_, in_=i_), "wbf_w", 16)
                        ci += 1
                for o_, i_ in conv_chunks[ci:]:
                    kb.inc(e.dma_start(out=o_, in_=i_), "wbf_w", 16)

            @blk.scalar
            def _(e):
                e.wait_ge(kb.sem("cst_z"), 2)
                for n0 in range(0, NB, 2):
                    pr = n0 // 2
                    r, h, qi, ki, dg = blocks[n0]
                    e.wait_ge(kb.sem("sc_mm"), n0 + 2)
                    prevs = [slot_prev_av[n0], slot_prev_av[n0 + 1]]
                    prevs = [p for p in prevs if p is not None]
                    if prevs:
                        e.wait_ge(kb.sem("av_mm"), max(prevs) + 1)
                    s0 = slot_of(n0)
                    kb.inc(e.activation(PPROB[:, s0 * 512: s0 * 512 + 1024],
                                        SCP2[pr % 2][:], AF.Exp,
                                        bias=CONSTT[:, 0:1], scale=EXP_SCALE), "pex")

            @blk.vector
            def _(e):
                avend = {}
                for n, (r, h, qi, ki, dg) in enumerate(blocks):
                    if ki == 4 * qi + 3:
                        avend[r] = n + 1
                for n, (r, h, qi, ki, dg) in enumerate(blocks):
                    ot, rb = h // 2, 64 * (h % 2)
                    if ki == 4 * qi + 3:
                        e.wait_ge(kb.sem("av_mm"), avend[r])
                        e.drain()
                        kb.inc(e.reciprocal(RCP[64:65, :], AVP[r % 2][64:65, :]), "rc_ev")
                        e.wait_ge(kb.sem("bc_mm"), r + 1)
                        e.drain()
                        kb.inc(e.tensor_copy(BCS[:], BCP[:]), None)
                        e.drain()
                        kb.inc(e.scalar_tensor_tensor(
                            OT[rb:rb + 64, ot * T + qi * 512: ot * T + qi * 512 + 512],
                            AVP[r % 2][0:64, :], 1.0, BCS[:], ALU.mult, ALU.mult),
                            "ot_ev")

        if stop_after < 3:
            return _epilog(nc, kb, blk, out_t, locals(), dbg_names, dbg_handles)

        p4 = ExitStack()
        with p4:
            FIN = [p4.enter_context(nc.psum_tensor(f"fin{i}", [128, 512], F32))
                   for i in range(2)]

            @blk.sync
            def _(e):
                e.wait_ge(kb.sem("sc_mm"), NB)
                for tt in range(TT):
                    if tt >= 2:
                        e.wait_ge(kb.sem("r1_ev"), (tt - 1) * 2)
                    kb.inc(e.dma_start(out=XRES[tt % 2][:],
                                       in_=x_in[tt * 128:(tt + 1) * 128, :]),
                           f"xres_l{tt % 2}", 16)

            @blk.tensor
            def _(e):
                e.wait_ge(kb.sem("ot_ev"), len(rounds))
                e.wait_ge(kb.sem("wt_l1"), 16 * DC * 2)
                n = 0
                for tt in range(TT):
                    for pc in range(2):
                        ps = FIN[n % 2]
                        if n >= 2:
                            e.wait_ge(kb.sem("r1_ev"), n - 1)
                        for oc in range(DC):
                            mm = e.matmul(ps[:], OT[:, oc * T + tt * 128: oc * T + tt * 128 + 128],
                                          WT1[:, oc * D + pc * 512: oc * D + pc * 512 + 512],
                                          start=(oc == 0), stop=(oc == DC - 1))
                        kb.inc(mm, "fin_mm")
                        n += 1

            @blk.vector
            def _(e):
                n = 0
                for tt in range(TT):
                    if tt >= 2:
                        e.wait_ge(kb.sem(f"r2_w{tt % 2}"), 16 * (tt // 2))
                        e.wait_ge(kb.sem("xln_c"), tt - 1)
                    for pc in range(2):
                        e.wait_ge(kb.sem("fin_mm"), n + 1)
                        e.wait_ge(kb.sem(f"xres_l{tt % 2}"), 16 * (tt // 2 + 1))
                        kb.inc(e.scalar_tensor_tensor(
                            R1T[tt % 2][:, pc * 512:(pc + 1) * 512], FIN[n % 2][:],
                            1.0 / 32.0, XRES[tt % 2][:, pc * 512:(pc + 1) * 512],
                            ALU.mult, ALU.add), "r1_ev")
                        n += 1
                    e.drain()
                    kb.inc(e.bn_stats(BN6[:, 0:6], R1T[tt % 2][:, 0:512]), None)
                    e.drain()
                    kb.inc(e.bn_stats(BN6[:, 6:12], R1T[tt % 2][:, 512:1024]), None)
                    e.drain()
                    kb.inc(e.bn_aggr(MV2[:], BN6[:]), "ln1_ba")
                    e.drain()
                    kb.inc(e.tensor_copy(MSTAT[:, tt:tt + 1], MV2[:, 0:1]), None)
                    e.wait_ge(kb.sem("ln1_sq"), tt + 1)
                    e.drain()
                    kb.inc(e.reciprocal(RSTAT[:, tt:tt + 1], SDV[:]), None)
                    e.drain()
                    kb.inc(e.tensor_scalar(
                        R1T[tt % 2][:], R1T[tt % 2][:], MSTAT[:, tt:tt + 1],
                        RSTAT[:, tt:tt + 1], ALU.subtract, ALU.mult), "ln1_ap")
                    e.drain()
                    kb.inc(e.tensor_copy(XLN[:, tt * D:(tt + 1) * D], R1T[tt % 2][:]),
                           "xln_c")

            @blk.scalar
            def _(e):
                for tt in range(TT):
                    e.wait_ge(kb.sem("ln1_ba"), tt + 1)
                    kb.inc(e.activation(SDV[:], MV2[:, 1:2], AF.Sqrt, bias=CONSTT[:, 1:2]),
                           "ln1_sq")

            @blk.gpsimd
            def _(e):
                for tt in range(TT):
                    e.wait_ge(kb.sem("ln1_ap"), tt + 1)
                    e.wait_ge(kb.sem("xln_c"), tt + 1)
                    kb.inc(e.dma_start(out=resid2_d[tt * 128:(tt + 1) * 128, :],
                                       in_=R1T[tt % 2][:]), f"r2_w{tt % 2}", 16)

        if stop_after < 4:
            return _epilog(nc, kb, blk, out_t, locals(), dbg_names, dbg_handles)

        p5 = ExitStack()
        with p5:
            ORW = [p5.enter_context(nc.psum_tensor(f"orw{i}", [128, E], F32))
                   for i in range(2)]

            @blk.tensor
            def _(e):
                e.wait_ge(kb.sem("r1_ev"), 2 * TT)
                for tt in range(TT):
                    if tt >= 2:
                        e.wait_ge(kb.sem("rsb_ev"), tt - 1)
                    for oc in range(DC):
                        mm = e.matmul(ORW[tt % 2][:],
                                      OT[:, oc * T + tt * 128: oc * T + tt * 128 + 128],
                                      RWPT[:, oc * E:(oc + 1) * E],
                                      start=(oc == 0), stop=(oc == DC - 1))
                    kb.inc(mm, "orw_mm")

            @blk.vector
            def _(e):
                e.wait_ge(kb.sem("ln1_ap"), TT)
                e.wait_ge(kb.sem("srwb_ev"), 1)
                for tt in range(TT):
                    e.wait_ge(kb.sem("orw_mm"), tt + 1)
                    kb.inc(e.tensor_scalar(MS8[:], SRWB[:, 0:E], MSTAT[:, tt:tt + 1],
                                           None, ALU.mult), None)
                    e.drain()
                    kb.inc(e.scalar_tensor_tensor(LG8[:], ORW[tt % 2][:], 1.0 / 32.0,
                                                  xraw_sb[:, tt * E:(tt + 1) * E],
                                                  ALU.mult, ALU.add), None)
                    e.drain()
                    kb.inc(e.tensor_sub(LG8[:], LG8[:], MS8[:]), None)
                    e.drain()
                    kb.inc(e.tensor_scalar(LG8[:], LG8[:], RSTAT[:, tt:tt + 1], None,
                                           ALU.mult), None)
                    e.drain()
                    kb.inc(e.reduce_max(NMX[:], LG8[:], AX.X, negate=True), "r_mx")
                    e.wait_ge(kb.sem("r_ex"), tt + 1)
                    e.drain()
                    kb.inc(e.reciprocal(SMR[:], SMS[:]), None)
                    e.drain()
                    kb.inc(e.tensor_scalar(R_sb[:, tt * E:(tt + 1) * E], LG8[:], SMR[:],
                                           None, ALU.mult), "rsb_ev")

            @blk.scalar
            def _(e):
                for tt in range(TT):
                    e.wait_ge(kb.sem("r_mx"), tt + 1)
                    kb.inc(e.activation(LG8[:], LG8[:], AF.Exp, bias=NMX[:],
                                        accum_out=SMS[:]), "r_ex")

        p5b = ExitStack()
        with p5b:
            RTP = [p5b.enter_context(nc.psum_tensor(f"rtp{i}", [E, 128], F32))
                   for i in range(2)]
            REPP = [p5b.enter_context(nc.psum_tensor(f"repp{i}", [128, 512], F32))
                    for i in range(2)]

            @blk.tensor
            def _(e):
                e.wait_ge(kb.sem("rsb_ev"), TT)
                for tt in range(TT):
                    if tt >= 2:
                        e.wait_ge(kb.sem("rt_ev"), tt - 1)
                    kb.inc(e.transpose(RTP[tt % 2][:],
                                       R_sb[:, tt * E:(tt + 1) * E], IDF[:]), "rt_mm")

            @blk.vector
            def _(e):
                for tt in range(TT):
                    e.wait_ge(kb.sem("rt_mm"), tt + 1)
                    kb.inc(e.tensor_copy(RT[:, tt * 128:(tt + 1) * 128], RTP[tt % 2][:]),
                           "rt_ev")

            @blk.tensor
            def _(e):
                e.wait_ge(kb.sem("rt_ev"), TT)
                for qc in range(4):
                    if qc >= 2:
                        e.wait_ge(kb.sem("rep_ev"), qc - 1)
                    kb.inc(e.matmul(REPP[qc % 2][:], BREP[:],
                                    RT[:, qc * 512:(qc + 1) * 512],
                                    start=True, stop=True), "rep_mm")

            @blk.vector
            def _(e):
                for qc in range(4):
                    e.wait_ge(kb.sem("rep_mm"), qc + 1)
                    kb.inc(e.tensor_copy(REPS[:, qc * 512:(qc + 1) * 512],
                                         REPP[qc % 2][:]), "rep_ev")

        if stop_after < 5:
            return _epilog(nc, kb, blk, out_t, locals(), dbg_names, dbg_handles)

        p6 = ExitStack()
        with p6:
            BIS = p6.enter_context(nc.psum_tensor("bis", [128, 2], F32))
            CN8 = p6.enter_context(nc.psum_tensor("cn8", [E, 1], F32))

            @blk.vector
            def _(e):
                e.wait_ge(kb.sem("rep_ev"), 4)
                kb.inc(e.memset(LOW8[:, 0:1], 0.0), "bi_init")
                kb.inc(e.memset(LOW8[:, 1:2], 1.0), "bi_init")

            for it in range(NBIS):
                @blk.tensor
                def _(e, it=it):
                    if it == 0:
                        e.wait_ge(kb.sem("bi_init"), 2)
                    else:
                        e.wait_ge(kb.sem("bi_up"), it * 2)
                    kb.inc(e.matmul(BIS[:], BREP[:], LOW8[:], start=True, stop=True),
                           "bi_rep")

                @blk.vector
                def _(e, it=it):
                    e.wait_ge(kb.sem("bi_rep"), it + 1)
                    kb.inc(e.tensor_copy(LWR[:], BIS[:]), None)
                    e.drain()
                    kb.inc(e.tensor_mul(TAU2[:], LWR[:, 1:2], JCOL[:]), None)
                    e.drain()
                    kb.inc(e.tensor_scalar(TAU2[:], TAU2[:], 1.0 / 17.0, None, ALU.mult),
                           None)
                    e.drain()
                    kb.inc(e.tensor_add(TAU[:], TAU2[:], LWR[:, 0:1]), None)
                    e.drain()
                    kb.inc(e.tensor_copy(TAUD[:, it:it + 1], TAU[:]), None)
                    e.drain()
                    kb.inc(e.tensor_scalar(BSCR[:], REPS[:], TAU[:], None, ALU.is_ge,
                                           op1=ALU.add, accum_out=CNT1[:]), None)
                    e.drain()
                    kb.inc(e.tensor_copy(CNTD[:, it:it + 1], CNT1[:]), None)
                    e.drain()
                    kb.inc(e.tensor_scalar(FLG1[:], CNT1[:], float(CAP) - 0.5, None,
                                           ALU.is_ge), "bi_fl")

                @blk.tensor
                def _(e, it=it):
                    e.wait_ge(kb.sem("bi_fl"), it + 1)
                    kb.inc(e.matmul(CN8[:], BSEG[:], FLG1[:], start=True, stop=True),
                           "bi_js")

                @blk.vector
                def _(e, it=it):
                    e.wait_ge(kb.sem("bi_js"), it + 1)
                    kb.inc(e.tensor_copy(JS8[:], CN8[:]), None)
                    e.drain()
                    kb.inc(e.tensor_copy(JSD[:, it:it + 1], JS8[:]), None)
                    e.drain()
                    kb.inc(e.tensor_mul(JW8[:], JS8[:], LOW8[0:E, 1:2]), None)
                    e.drain()
                    kb.inc(e.tensor_scalar(JW8[:], JW8[:], 1.0 / 17.0, None, ALU.mult),
                           None)
                    e.drain()
                    kb.inc(e.tensor_add(LOW8[0:E, 0:1], LOW8[0:E, 0:1], JW8[:]), "bi_up")
                    kb.inc(e.tensor_scalar(LOW8[0:E, 1:2], LOW8[0:E, 1:2], 1.0 / 17.0,
                                           None, ALU.mult), "bi_up")

        if stop_after < 6:
            return _epilog(nc, kb, blk, out_t, locals(), dbg_names, dbg_handles)

        p7 = ExitStack()
        with p7:
            TRX = [p7.enter_context(nc.psum_tensor(f"trx{i}", [128, E], F32))
                   for i in range(2)]
            B1P = [p7.enter_context(nc.psum_tensor(f"b1p{i}", [128, E], F32))
                   for i in range(2)]

            @blk.vector
            def _(e):
                e.wait_ge(kb.sem("bi_up"), 2 * NBIS)
                kb.inc(e.tensor_scalar(MSK[:], RT[:], LOW8[0:E, 0:1], None, ALU.is_ge),
                       None)
                e.drain()
                kb.inc(e.tensor_tensor_scan(CS[:], MSK[:], MSK[:], 0.0, ALU.add,
                                            ALU.bypass), None)
                e.drain()
                kb.inc(e.tensor_mul(CS[:], CS[:], MSK[:]), "sel_ct")

            @blk.tensor
            def _(e):
                e.wait_ge(kb.sem("sel_ct"), 1)
                for tt in range(TT):
                    if tt >= 2:
                        e.wait_ge(kb.sem("ctt_ev"), tt - 1)
                    kb.inc(e.transpose(TRX[tt % 2][:], CS[:, tt * 128:(tt + 1) * 128],
                                       IDF[0:E, 0:E]), "ctt_mm")

            @blk.vector
            def _(e):
                for tt in range(TT):
                    e.wait_ge(kb.sem("ctt_mm"), tt + 1)
                    if tt == TT - 1:
                        kb.inc(e.tensor_tensor(GME[:], MSK[:], RT[:], ALU.mult),
                               "sel_gme")
                        e.drain()
                    kb.inc(e.tensor_copy(CTT[:, tt * E:(tt + 1) * E], TRX[tt % 2][:]),
                           "ctt_ev")
                gbl = GHL[:].rearrange("p (c e u) -> p c e u", e=E, u=4)
                g3 = GMT[:].rearrange("p (t e) -> p t e", e=E)
                e.drain()
                kb.inc(e.tensor_scalar(GMT[:], CTT[:], 0.0, None, ALU.is_gt), None)
                e.drain()
                kb.inc(e.tensor_mul(GMT[:], GMT[:], R_sb[:]), None)
                e.drain()
                kb.inc(e.tensor_copy(gbl[:, :, :, 0:1],
                                     g3.rearrange("p t e -> p t e ()")), None)
                e.drain()
                kb.inc(e.scalar_tensor_tensor(
                    gbl[:, :, :, 1:2], gbl[:, :, :, 0:1], -1.0,
                    g3.rearrange("p t e -> p t e ()"), ALU.mult, ALU.add), None)
                e.drain()
                kb.inc(e.tensor_copy(
                    gbl[:, :, :, 2:3],
                    PCCOL[:, 0:1].rearrange("p (t x) -> p t x ()", t=1)
                    .to_broadcast([128, TT, E, 1])), None)
                e.drain()
                kb.inc(e.tensor_copy(
                    gbl[:, :, :, 3:4],
                    PCCOL[:, 2:2 + TT].rearrange("p t -> p t () ()")
                    .to_broadcast([128, TT, E, 1])), "gm_c")

            @blk.sync
            def _(e):
                e.wait_ge(kb.sem("sel_gme"), 1)
                e.wait_ge(kb.sem("ctt_mm"), TT)
                kb.inc(e.dma_start(out=B1F[:], in_=b1_in[:]), "b1_l", 16)
                kb.inc(e.dma_start(out=B2F[:], in_=b2_in[:]), "b2_l", 16)
                kb.inc(e.dma_start(out=IOTA[:], in_=iota_in[:]), "iota_l", 16)

            @blk.vector
            def _(e):
                e.wait_ge(kb.sem("b2_l"), 16)
                kb.inc(e.tensor_copy(B2B[:], B2F[:]), "b2b_c")
                kb.inc(e.memset(R4[:], 0.0), None)

            @blk.tensor
            def _(e):
                e.wait_ge(kb.sem("b1_l"), 16)
                for ht in range(HT):
                    if ht >= 2:
                        e.wait_ge(kb.sem("b1t_ev"), ht - 1)
                    kb.inc(e.transpose(B1P[ht % 2][:], B1F[:, ht * 128:(ht + 1) * 128],
                                       IDF[0:E, 0:E]), "b1t_mm")

            @blk.vector
            def _(e):
                for ht in range(HT):
                    e.wait_ge(kb.sem("b1t_mm"), ht + 1)
                    kb.inc(e.tensor_copy(B1T[:, ht * E:(ht + 1) * E], B1P[ht % 2][:]),
                           "b1t_ev")

            @blk.gpsimd
            def _(e):
                e.wait_ge(kb.sem("sel_gme"), 1)
                e.wait_ge(kb.sem("r2_w0"), 16 * (TT // 2))
                e.wait_ge(kb.sem("r2_w1"), 16 * (TT // 2))
                kb.inc(e.memset(HTB[:], 0.0), "htb_z")

        if stop_after < 7:
            return _epilog(nc, kb, blk, out_t, locals(), dbg_names, dbg_handles)

        p8 = ExitStack()
        with p8:
            XGP1 = p8.enter_context(nc.psum_tensor("xgp", [128, 1024], F32))
            XGP = [XGP1[:, 0:SH], XGP1[:, 512:512 + SH]]
            H2T = [p8.enter_context(nc.psum_tensor(f"h2p{i}", [128, 1024], F32))
                   for i in range(2)]
            H2P = [H2T[0][:, 0:SH], H2T[0][:, 512:512 + SH],
                   H2T[1][:, 0:SH], H2T[1][:, 512:512 + SH]]
            Y2P = [p8.enter_context(nc.psum_tensor(f"y2p{i}", [128, DWIN], F32))
                   for i in range(2)]
            TR4 = XGP1[:, 800:804]

            NHW = FF // HWIN
            NDW = D // DWIN

            for ex in range(E):
                @blk.vector
                def _(e, ex=ex):
                    if ex == 0:
                        e.wait_ge(kb.sem("iota_l"), 16)
                        e.wait_ge(kb.sem("ctt_ev"), TT)
                        e.wait_ge(kb.sem("rep_ev"), 4)
                    else:
                        e.wait_ge(kb.sem("g_mm"), kb.val("g_mm"))
                    for c in range(TT):
                        kb.inc(e.tensor_scalar(
                            PT[:, c * S:(c + 1) * S], IOTA[:],
                            CTT[:, c * E + ex: c * E + ex + 1], None,
                            ALU.is_equal), "pt_b")

                base_xgev = kb.val("xg_ev")
                base_m1mm_prev = kb.val("m1_mm")

                @blk.tensor
                def _(e, ex=ex, base_xgev=base_xgev):
                    e.wait_ge(kb.sem("pt_b"), kb.val("pt_b"))
                    if ex == 0:
                        e.wait_ge(kb.sem("xln_c"), TT)
                        e.wait_ge(kb.sem("gm_c"), 1)
                        e.wait_ge(kb.sem("htb_z"), 1)
                        e.wait_ge(kb.sem("b1t_ev"), HT)
                    e.wait_ge(kb.sem("gel_ev"), kb.val("gel_ev"))
                    e.wait_ge(kb.sem("sl_cp"), kb.val("sl_cp"))
                    GB = [XGP1, H2T[0], H2T[1]]
                    for dt in range(DC):
                        if dt >= 3:
                            e.wait_ge(kb.sem("xg_ev"), base_xgev + dt - 2)
                        gb = GB[dt % 3]
                        for c in range(TT):
                            m1 = e.matmul(gb[:, 0:SH],
                                          XLN[:, c * D + dt * 128: c * D + dt * 128 + 128],
                                          PT[:, c * S: c * S + SH],
                                          start=(c == 0), stop=(c == TT - 1))
                            m2 = e.matmul(gb[:, 512:512 + SH],
                                          XLN[:, c * D + dt * 128: c * D + dt * 128 + 128],
                                          PT[:, c * S + SH: c * S + 2 * SH],
                                          start=(c == 0), stop=(c == TT - 1))
                        kb.inc(m2, "g_mm")
                    e.wait_ge(kb.sem("xg_ev"), base_xgev + DC)
                    for c in range(TT):
                        m1 = e.matmul(XGP1[0:4, 0:SH],
                                      GHL[:, (c * E + ex) * 4:(c * E + ex) * 4 + 4],
                                      PT[:, c * S: c * S + SH],
                                      start=(c == 0), stop=(c == TT - 1))
                        m2 = e.matmul(XGP1[0:4, 512:512 + SH],
                                      GHL[:, (c * E + ex) * 4:(c * E + ex) * 4 + 4],
                                      PT[:, c * S + SH: c * S + 2 * SH],
                                      start=(c == 0), stop=(c == TT - 1))
                    kb.inc(m2, "g_mm")

                @blk.vector
                def _(e, ex=ex, base_xgev=base_xgev, base_m1mm_prev=base_m1mm_prev):
                    gbase = kb.val("g_mm") - (DC + 1)
                    e.wait_ge(kb.sem("m1_mm"), base_m1mm_prev)
                    GBv = [XGP1, H2T[0], H2T[1]]
                    for dt in range(DC):
                        e.wait_ge(kb.sem("g_mm"), gbase + dt + 1)
                        kb.inc(e.tensor_copy(
                            XGT[:, dt * S: dt * S + 2 * SH].rearrange(
                                "p (h x) -> p h x", x=SH),
                            GBv[dt % 3][:].rearrange(
                                "p (h x) -> p h x", x=512)[:, :, 0:SH]), "xg_ev")
                    e.wait_ge(kb.sem("g_mm"), gbase + DC + 1)
                    e.wait_ge(kb.sem("sl_mm"), kb.val("sl_mm"))
                    kb.inc(e.tensor_copy(
                        R4[:, 0:2 * SH].rearrange("p (h x) -> p h x", x=SH),
                        XGP1[0:4, :].rearrange(
                            "p (h x) -> p h x", x=512)[:, :, 0:SH]), "r4_ev")

                base_slcp = kb.val("sl_cp")

                @blk.tensor
                def _(e, ex=ex, base_slcp=base_slcp):
                    e.wait_ge(kb.sem("r4_ev"), kb.val("r4_ev"))
                    for st in range(NST):
                        if st >= 1:
                            e.wait_ge(kb.sem("sl_cp"), base_slcp + st)
                        kb.inc(e.transpose(TR4, R4[:, st * 128:(st + 1) * 128],
                                           IDF[0:4, 0:4]), "sl_mm")

                @blk.vector
                def _(e, ex=ex):
                    base = kb.val("sl_mm") - NST
                    if ex >= 2:
                        e.wait_ge(kb.sem("y_ev"), NDW * NST * (ex - 1))
                        e.wait_ge(kb.sem("sc_w"), 16 * NST * 2 * (ex - 1))
                    for st in range(NST):
                        e.wait_ge(kb.sem("sl_mm"), base + st + 1)
                        kb.inc(e.tensor_copy(SLS[:], TR4), "sl_cp")
                        e.drain()
                        kb.inc(e.tensor_add(GSB[ex % 2][:, st:st + 1],
                                            SLS[:, 0:1], SLS[:, 1:2]), "sl_gs")
                        kb.inc(e.scalar_tensor_tensor(FIDX[:], SLS[:, 3:4], 128.0,
                                                      SLS[:, 2:3], ALU.mult, ALU.add),
                               None)
                        e.drain()
                        kb.inc(e.tensor_copy(IDXCB[ex % 2][:, st:st + 1], FIDX[:]),
                               "sl_ix")

                @blk.sync
                def _(e, ex=ex):
                    if ex == 0:
                        e.wait_ge(kb.sem("wbf_w"), 128 + 16 * 128)
                    for hw in range(NHW):
                        w = ex * NHW + hw
                        if w >= 2:
                            e.wait_ge(kb.sem("m1_mm"), 2 * (w - 1))
                        kb.inc(e.dma_start(
                            out=W1S[w % 2][:].rearrange("p (c f) -> p c f", f=HWIN),
                            in_=w1bf_d[ex].rearrange("(c p) h -> p c h", p=128)[
                                :, :, hw * HWIN:(hw + 1) * HWIN]), f"w1_l{w % 2}", 16)

                base_gel = kb.val("gel_ev")

                @blk.tensor
                def _(e, ex=ex, base_gel=base_gel):
                    e.wait_ge(kb.sem("xg_ev"), kb.val("xg_ev"))
                    nht = 0
                    for hw in range(NHW):
                        w = ex * NHW + hw
                        e.wait_ge(kb.sem(f"w1_l{w % 2}"), 16 * (w // 2 + 1))
                        for hh in range(HWIN // 128):
                            a, b = H2P[2 * (nht % 2)], H2P[2 * (nht % 2) + 1]
                            if nht >= 2:
                                e.wait_ge(kb.sem("gel_ev"), base_gel + nht - 1)
                            for dc in range(DC):
                                lw = W1S[w % 2][:, dc * HWIN + hh * 128:
                                                dc * HWIN + hh * 128 + 128]
                                m1 = e.matmul(a, lw, XGT[:, dc * S: dc * S + SH],
                                              start=(dc == 0), stop=(dc == DC - 1))
                                m2 = e.matmul(b, lw, XGT[:, dc * S + SH: dc * S + 2 * SH],
                                              start=(dc == 0), stop=(dc == DC - 1))
                            kb.inc(m2, "m1_mm")
                            nht += 1

                base_m1 = kb.val("m1_mm") - HT
                base_m2mm_prev = kb.val("m2_mm")

                @blk.scalar
                def _(e, ex=ex, base_m1=base_m1, base_m2mm_prev=base_m2mm_prev):
                    e.wait_ge(kb.sem("m2_mm"), base_m2mm_prev)
                    for ht in range(HT):
                        e.wait_ge(kb.sem("m1_mm"), base_m1 + ht + 1)
                        bia = B1T[:, ht * E + ex: ht * E + ex + 1]
                        kb.inc(e.activation(
                            HTB[:, ht * SP: ht * SP + 2 * SH].rearrange(
                                "p (h x) -> p h x", x=SH),
                            H2T[ht % 2][:].rearrange(
                                "p (h x) -> p h x", x=512)[:, :, 0:SH],
                            AF.Gelu, bias=bia), "gel_ev")

                @blk.sync
                def _(e, ex=ex):
                    for dw in range(NDW):
                        u = ex * NDW + dw
                        if u >= 2:
                            e.wait_ge(kb.sem("m2_mm"), NST * (u - 1))
                        kb.inc(e.dma_start(
                            out=W2S[u % 2][:].rearrange("p (c f) -> p c f", f=DWIN),
                            in_=w2bf_d[ex].rearrange("(c p) d -> p c d", p=128)[
                                :, :, dw * DWIN:(dw + 1) * DWIN]), f"w2_l{u % 2}", 16)

                base_gel_end = kb.val("gel_ev")
                base_yev = kb.val("y_ev")

                @blk.tensor
                def _(e, ex=ex, base_gel_end=base_gel_end, base_yev=base_yev):
                    e.wait_ge(kb.sem("gel_ev"), base_gel_end)
                    nst = 0
                    for dw in range(NDW):
                        u = ex * NDW + dw
                        e.wait_ge(kb.sem(f"w2_l{u % 2}"), 16 * (u // 2 + 1))
                        for st in range(NST):
                            M = 128 if st < 4 else 32
                            ps = Y2P[nst % 2]
                            if nst >= 2:
                                e.wait_ge(kb.sem("y_ev"), base_yev + nst - 1)
                            for hc in range(HT):
                                mm = e.matmul(
                                    ps[0:M, :],
                                    HTB[:, hc * SP + st * 128: hc * SP + st * 128 + M],
                                    W2S[u % 2][:, hc * DWIN:(hc + 1) * DWIN],
                                    start=(hc == 0), stop=(hc == HT - 1))
                            kb.inc(mm, "m2_mm")
                            nst += 1
                        kb.inc(e.nop(), None)

                base_m2 = kb.val("m2_mm") - NDW * NST
                base_scw = kb.val("sc_w")

                @blk.vector
                def _(e, ex=ex, base_m2=base_m2, base_scw=base_scw):
                    e.wait_ge(kb.sem("sl_gs"), kb.val("sl_gs"))
                    n = 0
                    for dw in range(NDW):
                        dh, dq = dw // 2, dw % 2
                        for st in range(NST):
                            M = 128 if st < 4 else 32
                            e.wait_ge(kb.sem("m2_mm"), base_m2 + n + 1)
                            if dq == 0 and st == 0:
                                v = base_scw - 16 * NST * (1 - dh)
                                if v > 0:
                                    e.wait_ge(kb.sem("sc_w"), v)
                            kb.inc(e.tensor_scalar(
                                YBH[dh][0:M, st * 512 + dq * DWIN:
                                         st * 512 + (dq + 1) * DWIN],
                                Y2P[n % 2][0:M, :], GSB[ex % 2][0:M, st:st + 1], None,
                                ALU.mult), "y_ev")
                            n += 1

                base_yev2 = base_yev

                @blk.gpsimd
                def _(e, ex=ex, base_yev2=base_yev2):
                    e.wait_ge(kb.sem("sl_ix"), kb.val("sl_ix"))
                    if ex == 0:
                        e.wait_ge(kb.sem("r2_w0"), 16 * (TT // 2))
                        e.wait_ge(kb.sem("r2_w1"), 16 * (TT // 2))
                    for dh in range(2):
                        e.wait_ge(kb.sem("y_ev"), base_yev2 + (dh + 1) * 2 * NST)
                        for st in range(NST):
                            M = 128 if st < 4 else 32
                            e.wait_ge(kb.sem("sc_w"), kb.val("sc_w"))
                            kb.inc(e.indirect_dma_start(
                                out=resid2_d,
                                out_offset=IndirectOffsetOnAxis(
                                    IDXCB[ex % 2][0:M, st:st + 1], 0),
                                in_=YBH[dh][0:M, st * 512:(st + 1) * 512], in_offset=None,
                                element_offset=dh * 512,
                                compute_op=ALU.add), "sc_w", 16)

        if stop_after < 8:
            return _epilog(nc, kb, blk, out_t, locals(), dbg_names, dbg_handles)

        p9 = ExitStack()
        with p9:
            B2P = [p9.enter_context(nc.psum_tensor(f"b2p{i}", [128, 512], F32))
                   for i in range(2)]

            @blk.sync
            def _(e):
                e.wait_ge(kb.sem("sc_w"), kb.val("sc_w"))
                for tt in range(TT):
                    if tt >= 4:
                        e.wait_ge(kb.sem("o_ap"), tt - 3)
                    kb.inc(e.dma_start(out=R2T[tt % 4][:],
                                       in_=resid2_d[tt * 128:(tt + 1) * 128, :]),
                           f"r2_l{tt % 2}", 16)
                    if tt >= 1:
                        e.wait_ge(kb.sem("o_ap"), tt)
                        kb.inc(e.dma_start(out=out_t[(tt - 1) * 128:tt * 128, :],
                                           in_=OUTS[(tt - 1) % 4][:]),
                               f"out_w{(tt - 1) % 2}", 16)
                e.wait_ge(kb.sem("o_ap"), TT)
                kb.inc(e.dma_start(out=out_t[(TT - 1) * 128:TT * 128, :],
                                   in_=OUTS[(TT - 1) % 4][:]),
                       f"out_w{(TT - 1) % 2}", 16)
                e.wait_ge(kb.sem("out_w0"), kb.val("out_w0"))
                e.wait_ge(kb.sem("out_w1"), kb.val("out_w1"))

            @blk.tensor
            def _(e):
                e.wait_ge(kb.sem("sel_gme"), 1)
                e.wait_ge(kb.sem("b2b_c"), 1)
                e.wait_ge(kb.sem("y_ev"), kb.val("y_ev"))
                n = 0
                for tt in range(TT):
                    for pc in range(2):
                        if n >= 2:
                            e.wait_ge(kb.sem("b2t_ev"), n - 1)
                        kb.inc(e.matmul(B2P[n % 2][:], GME[:, tt * 128:(tt + 1) * 128],
                                        B2B[:, pc * 512:(pc + 1) * 512],
                                        start=True, stop=True), "b2t_mm")
                        n += 1

            @blk.vector
            def _(e):
                n = 0
                for tt in range(TT):
                    e.wait_ge(kb.sem(f"r2_l{tt % 2}"), 16 * (tt // 2 + 1))
                    for pc in range(2):
                        e.wait_ge(kb.sem("b2t_mm"), n + 1)
                        kb.inc(e.tensor_tensor(R2T[tt % 4][:, pc * 512:(pc + 1) * 512],
                                               R2T[tt % 4][:, pc * 512:(pc + 1) * 512],
                                               B2P[n % 2][:], ALU.add), "b2t_ev")
                        n += 1
                    e.drain()
                    kb.inc(e.bn_stats(BN6[:, 0:6], R2T[tt % 4][:, 0:512]), None)
                    e.drain()
                    kb.inc(e.bn_stats(BN6[:, 6:12], R2T[tt % 4][:, 512:1024]), None)
                    e.drain()
                    kb.inc(e.bn_aggr(MV2[:], BN6[:]), "ln2_ba")
                    e.wait_ge(kb.sem("ln2_sq"), tt + 1)
                    e.drain()
                    kb.inc(e.reciprocal(RSD2[:], SDV[:]), None)
                    e.drain()
                    if tt >= 4:
                        e.wait_ge(kb.sem(f"out_w{tt % 2}"), 16 * (tt // 2 - 1))
                    kb.inc(e.tensor_scalar(OUTS[tt % 4][:], R2T[tt % 4][:], MV2[:, 0:1],
                                           RSD2[:], ALU.subtract, ALU.mult), "o_ap")

            @blk.scalar
            def _(e):
                for tt in range(TT):
                    e.wait_ge(kb.sem("ln2_ba"), tt + 1)
                    kb.inc(e.activation(SDV[:], MV2[:, 1:2], AF.Sqrt, bias=CONSTT[:, 1:2]),
                           "ln2_sq")



        if dbg_names:
            return _epilog(nc, kb, blk, out_t, locals(), dbg_names, dbg_handles,
                           final=True)
    return nc, dbg_handles


def _epilog(nc, kb, blk, out_t, env, dbg_names, dbg_handles, final=False):
    snap = {k: v for k, v in kb.cnt.items() if v > 0}
    if dbg_names:
        @blk.sync
        def _(e):
            for k, v in snap.items():
                e.wait_ge(kb.sems[k], v)
            for name in dbg_names:
                t = env.get(name)
                assert t is not None, f"debug tensor {name} not found"
                shape = [int(s) for s in t.shape]
                h = nc.dram_tensor(f"dbg_{name}", shape, t.dtype,
                                   kind="ExternalOutput").ap()
                dbg_handles[name] = h
                kb.inc(e.dma_start(out=h, in_=t[:]), "dbg_w", 16)

    if not final:
        @blk.gpsimd
        def _(e):
            for k, v in snap.items():
                e.wait_ge(kb.sems[k], v)
            if dbg_names:
                e.wait_ge(kb.sem("dbg_w"), kb.val("dbg_w"))
            src = env["IDF"]
            kb.inc(e.dma_start(out=out_t[0:128, 0:128], in_=src[:]), "out_w", 16)
            e.wait_ge(kb.sem("out_w"), kb.val("out_w"))
    else:
        @blk.sync
        def _(e):
            e.wait_ge(kb.sem("dbg_w"), kb.val("dbg_w"))
    return nc, dbg_handles


def make_consts():
    c = {}
    c["id_bf"] = np.eye(128, dtype=ml_dtypes.bfloat16)
    c["id_f32"] = np.eye(128, dtype=np.float32)
    m = np.zeros((4, 128, 512), dtype=ml_dtypes.bfloat16)
    jj = np.arange(512)[None, :]
    pp = np.arange(128)[:, None]
    for cc in range(4):
        m[cc] = (jj >= pp + 128 * cc).astype(ml_dtypes.bfloat16)
    c["cmask"] = m
    c["iota_s"] = np.broadcast_to(np.arange(1, S + 1, dtype=np.float16), (128, S)).copy()
    c["jcol"] = ((np.arange(128) % 16) + 1).astype(np.float32).reshape(128, 1)
    c["brep"] = np.zeros((E, 128), np.float32)
    for ee in range(E):
        c["brep"][ee, ee * 16:(ee + 1) * 16] = 1.0
    c["bseg"] = np.ascontiguousarray(c["brep"].T)
    c["ones_f"] = np.ones((128, 128), np.float32)
    c["ones_b"] = np.ones((128, 128), ml_dtypes.bfloat16)
    pc = np.zeros((128, 2 + TT), dtype=ml_dtypes.bfloat16)
    pc[:, 0] = np.arange(128)
    for cc in range(TT):
        pc[:, 2 + cc] = cc
    c["pccol"] = pc
    return c


def make_inputs(L0_attn_x, L0_attn_W_key, L0_attn_W_query, L0_attn_W_value,
                L0_attn_W_proj, L0_ffn_router_w, L0_ffn_w1, L0_ffn_b1, L0_ffn_w2,
                L0_ffn_b2):
    consts = make_consts()
    shared = dict(
        wk=np.ascontiguousarray(L0_attn_W_key), wq=np.ascontiguousarray(L0_attn_W_query),
        wv=np.ascontiguousarray(L0_attn_W_value), wp=np.ascontiguousarray(L0_attn_W_proj),
        rw=np.ascontiguousarray(L0_ffn_router_w), w1=np.ascontiguousarray(L0_ffn_w1),
        b1=np.ascontiguousarray(L0_ffn_b1), w2=np.ascontiguousarray(L0_ffn_w2),
        b2=np.ascontiguousarray(L0_ffn_b2), **consts)
    xs = np.asarray(L0_attn_x)
    return [dict(x=np.ascontiguousarray(xs[c]), **shared) for c in range(xs.shape[0])]


def kernel(**inputs):
    nc, _ = build()
    in_maps = make_inputs(**inputs)
    res = run_bass_kernel_spmd(nc, in_maps, core_ids=list(range(8)))
    return np.stack([r["out"] for r in res.results], axis=0)


# version 20
# speedup vs baseline: 1.0270x; 1.0270x over previous
import numpy as np
import ml_dtypes
import concourse.bass as bass
import concourse.mybir as mybir
from concourse.bass_utils import run_bass_kernel_spmd
from concourse.bass_core_rust import IndirectOffsetOnAxis
from contextlib import ExitStack

F32 = mybir.dt.float32
BF16 = mybir.dt.bfloat16
FP16 = mybir.dt.float16
I32 = mybir.dt.int32
AF = mybir.ActivationFunctionType
ALU = mybir.AluOpType
AX = mybir.AxisListType

T, D, NH, DH, E, FF = 2048, 1024, 16, 64, 8, 4096
CAP = 513
S = CAP + 1
SH = 257
SP = 544
NST = 5
TT = T // 128
DC = D // 128
HT = FF // 128
NQC = 4
HWIN = 256
DWIN = 256
EXP_SCALE = 0.125
EXP_BIAS = -20.0
LN_EPS = 1e-5
NBIS = 13
KB_ = 1024
U = DH + 1


class KB:
    def __init__(self, nc, ctx):
        self.nc, self.ctx, self.sems, self.cnt = nc, ctx, {}, {}

    def sem(self, name):
        if name not in self.sems:
            self.sems[name] = self.ctx.enter_context(self.nc.semaphore(name))
            self.cnt[name] = 0
        return self.sems[name]

    def inc(self, inst, name, by=1):
        if name is None:
            return None
        inst.then_inc(self.sem(name), by)
        self.cnt[name] += by
        return self.cnt[name]

    def val(self, name):
        self.sem(name)
        return self.cnt[name]


def build(dbg_names=(), stop_after=99):
    nc = bass.Bass("TRN2", target_bir_lowering=False, debug=False,
                   dynamic_dma_scratch_size=16384)

    x_in = nc.dram_tensor("x", [T, D], F32, kind="ExternalInput").ap()
    w_in = {k: nc.dram_tensor(k, [D, D], F32, kind="ExternalInput").ap()
            for k in ("wk", "wq", "wv", "wp")}
    rw_in = nc.dram_tensor("rw", [E, D], F32, kind="ExternalInput").ap()
    w1_in = nc.dram_tensor("w1", [E, D, FF], F32, kind="ExternalInput").ap()
    b1_in = nc.dram_tensor("b1", [E, FF], F32, kind="ExternalInput").ap()
    w2_in = nc.dram_tensor("w2", [E, FF, D], F32, kind="ExternalInput").ap()
    b2_in = nc.dram_tensor("b2", [E, D], F32, kind="ExternalInput").ap()
    out_t = nc.dram_tensor("out", [T, D], F32, kind="ExternalOutput").ap()

    id_bf_in = nc.dram_tensor("id_bf", [128, 128], BF16, kind="ExternalInput").ap()
    id_f32_in = nc.dram_tensor("id_f32", [128, 128], F32, kind="ExternalInput").ap()
    cmask_in = nc.dram_tensor("cmask", [4, 128, 512], BF16, kind="ExternalInput").ap()
    iota_in = nc.dram_tensor("iota_s", [128, S], FP16, kind="ExternalInput").ap()
    jcol_in = nc.dram_tensor("jcol", [128, 1], F32, kind="ExternalInput").ap()
    brep_in = nc.dram_tensor("brep", [E, 128], F32, kind="ExternalInput").ap()
    bseg_in = nc.dram_tensor("bseg", [128, E], F32, kind="ExternalInput").ap()
    ones_in = nc.dram_tensor("ones_f", [128, 128], F32, kind="ExternalInput").ap()
    onesb_in = nc.dram_tensor("ones_b", [128, 128], BF16, kind="ExternalInput").ap()
    pccol_in = nc.dram_tensor("pccol", [128, 2 + TT], BF16, kind="ExternalInput").ap()

    xbf_d = nc.dram_tensor("xbf_d", [T, D], BF16, kind="Internal").ap()
    wbf_d = nc.dram_tensor("wbf_d", [4, D, D], BF16, kind="Internal").ap()
    resid2_d = nc.dram_tensor("resid2_d", [T, D], F32, kind="Internal").ap()
    w1bf_d = nc.dram_tensor("w1bf_d", [E, D, FF], BF16, kind="Internal").ap()
    w2bf_d = nc.dram_tensor("w2bf_d", [E, FF, D], BF16, kind="Internal").ap()

    dbg_handles = {}
    ctx = ExitStack()
    with ctx:
        sb = nc.alloc_sbuf_tensor
        IDB = sb("IDB", [128, 128], BF16)
        IDF = sb("IDF", [128, 128], F32)
        ONESF = sb("ONESF", [128, 128], F32)
        ONESB = sb("ONESB", [128, 128], BF16)
        JCOL = sb("JCOL", [128, 1], F32)
        BREP = sb("BREP", [E, 128], F32)
        BSEG = sb("BSEG", [128, E], F32)
        PCCOL = sb("PCCOL", [128, 2 + TT], BF16)
        RWT = sb("RWT", [128, DC * E], BF16)
        RWPT = sb("RWPT", [128, DC * E], BF16)
        SRWB = sb("SRWB", [128, E], F32)
        xraw_sb = sb("XRAW", [128, TT * E], F32)
        LOW8 = sb("LOW8", [E, 2], F32)
        BN6 = sb("BN6", [128, 12], F32)
        MV2 = sb("MV2", [128, 2], F32)
        SDV = sb("SDV", [128, 1], F32)
        RSD2 = sb("RSD2", [128, 1], F32)
        LG8 = sb("LG8", [128, E], F32)
        MS8 = sb("MS8", [128, E], F32)
        NMX = sb("NMX", [128, 1], F32)
        SMS = sb("SMS", [128, 1], F32)
        SMR = sb("SMR", [128, 1], F32)
        CNT1 = sb("CNT1", [128, 1], F32)
        FLG1 = sb("FLG1", [128, 1], F32)
        LWR = sb("LWR", [128, 2], F32)
        TAU = sb("TAU", [128, 1], F32)
        TAU2 = sb("TAU2", [128, 1], F32)
        JS8 = sb("JS8", [E, 1], F32)
        JW8 = sb("JW8", [E, 1], F32)
        TMP8 = sb("TMP8", [128, E], F32)
        CONSTT = sb("CONSTT", [128, 2], F32)
        CNTD = sb("CNTD", [128, NBIS], F32)
        TAUD = sb("TAUD", [128, NBIS], F32)
        JSD = sb("JSD", [E, NBIS], F32)

        TOP = nc.sbuf_top
        szA, szB, szC, szD, szE, szF, szG = (33 * KB_ + 512, 32 * KB_, 32 * KB_,
                                             26 * KB_, 32 * KB_, 32 * KB_, 16 * KB_)
        base = TOP - (szA + szB + szC + szD + szE + szF + szG)
        offA = base
        offB = offA + szA
        offC = offB + szB
        offD = offC + szC
        offE = offD + szD
        offF = offE + szE
        offG = offF + szF

        at = nc.alloc_sbuf_tensor_at

        V = at("V", [128, TT * NH * U], BF16, offset=offA)
        R2T = [at(f"R2T{i}", [128, D], F32, offset=offA + i * 4 * KB_) for i in range(2)]
        R2T += [at(f"R2T{2 + i}", [128, D], F32, offset=offD + i * 4 * KB_)
                for i in range(2)]
        OUTS2 = [at(f"OUTS{2 + i}", [128, D], F32, offset=offD + (2 + i) * 4 * KB_)
                 for i in range(2)]
        OUTS = [at(f"OUTS{i}", [128, D], F32, offset=offA + (2 + i) * 4 * KB_)
                for i in range(2)]
        OUTS += OUTS2
        IOTA = at("IOTA", [128, S], FP16, offset=offA + 16 * KB_)
        R_sb = at("R_sb", [128, TT * E], F32, offset=offA + 18 * KB_)
        MSTAT = at("MSTAT", [128, TT], F32, offset=offA + 19 * KB_)
        RSTAT = at("RSTAT", [128, TT], F32, offset=offA + 19 * KB_ + 128)
        GMT = at("GMT", [128, TT * E], F32, offset=offA + 20 * KB_)
        CTT = at("CTT", [128, TT * E], F32, offset=offA + 21 * KB_)
        QT = at("QT", [128, DC * T], BF16, offset=offB)
        RWB16 = at("RWB16", [E, D], BF16, offset=offB)
        RT = at("RT", [E, T], F32, offset=offB)
        REPS = at("REPS", [128, T], F32, offset=offB + 8 * KB_)
        R1T = [at(f"R1T{i}", [128, D], F32, offset=offB + 16 * KB_ + i * 4 * KB_)
               for i in range(2)]
        XRES = [at(f"XRES{i}", [128, D], F32, offset=offB + 24 * KB_ + i * 4 * KB_)
                for i in range(2)]
        HTB = at("HTB", [128, HT * SP], BF16, offset=offB)
        KT = at("KT", [128, DC * T], BF16, offset=offC)
        BSCR = at("BSCR", [128, T], F32, offset=offC + 2 * KB_)
        MSK = at("MSK", [E, T], F32, offset=offC + 10 * KB_)
        CS = at("CS", [E, T], F32, offset=offC + 18 * KB_)
        B1F = at("B1F", [E, FF], F32, offset=offC + 2 * KB_)
        W2S = [at("W2S0", [128, HT * DWIN], BF16, offset=offC + 2 * KB_),
               None]
        B2F = at("B2F", [E, D], F32, offset=offC + 18 * KB_)
        GME = at("GME", [E, T], BF16, offset=offA + 22 * KB_)
        B1T = at("B1T", [128, HT * E], F32, offset=offA + 26 * KB_)
        B2B = at("B2B", [E, D], BF16, offset=offA + 27 * KB_)
        RWF = at("RWF", [E, D], F32, offset=offD)
        WPN = at("WPN", [128, DC * D], BF16, offset=offD + 4 * KB_)
        PPROB = at("PPROB", [128, 8 * 512], BF16, offset=offD + 4 * KB_)
        RCP = at("RCP", [65, 512], F32, offset=offD + 12 * KB_)
        BCS = at("BCS", [64, 512], F32, offset=offD + 14 * KB_)
        CMASK = at("CMASK", [128, 4 * 512], BF16, offset=offD + 16 * KB_)
        YB = at("YB", [128, NST * (D // 2)], F32, offset=offD)
        YB2 = at("YB2", [128, NST * (D // 2)], F32, offset=offA)
        YBH = [YB, YB2]
        W1S = [at("W1S0", [128, DC * HWIN], BF16, offset=offA + 29 * KB_),
               None]
        W2S[1] = at("W2S1", [128, HT * DWIN], BF16, offset=offD + 10 * KB_)
        XT = at("XT", [128, DC * T], BF16, offset=offE)
        OT = at("OT", [128, DC * T], BF16, offset=offE)
        PT = at("PT", [128, TT * S], BF16, offset=offE)
        XGT = at("XGT", [128, DC * S], BF16, offset=offE + 17 * KB_)
        W1S[1] = at("W1S1", [128, DC * HWIN], BF16, offset=offE + 26 * KB_)
        WSTG = at("WSTG", [128, 4 * D], BF16, offset=offF)
        WT0 = at("WT0", [128, DC * D], BF16, offset=offF + 8 * KB_)
        XLN = at("XLN", [128, TT * D], BF16, offset=offF)
        WT1 = at("WT1", [128, DC * D], BF16, offset=offG)
        GHL = at("GHL", [128, TT * E * 4], BF16, offset=offG)
        R4 = at("R4", [4, NST * 128], F32, offset=offG + 2 * KB_)
        SLS = at("SLS", [128, 4], F32, offset=offG + 5 * KB_)
        GS = at("GS", [128, NST], F32, offset=offG + 5 * KB_ + 128)
        FIDX = at("FIDX", [128, 1], F32, offset=offG + 6 * KB_)
        IDXC = at("IDXC", [128, NST], I32, offset=offG + 6 * KB_ + 128)
        GS2 = at("GS2", [128, NST], F32, offset=offG + 7 * KB_)
        IDXC2 = at("IDXC2", [128, NST], I32, offset=offG + 7 * KB_ + 128)
        GSB = [GS, GS2]
        IDXCB = [IDXC, IDXC2]
        RWPE = at("RWPE", [E, D], BF16, offset=offD + 20 * KB_)

        blk = ctx.enter_context(nc.Block())
        kb = KB(nc, ctx)

        @blk.sync
        def _(e):
            for dst, src in [(IDB, id_bf_in), (IDF, id_f32_in), (ONESF, ones_in),
                             (ONESB, onesb_in), (JCOL, jcol_in), (BREP, brep_in),
                             (BSEG, bseg_in), (PCCOL, pccol_in), (RWF, rw_in)]:
                kb.inc(e.dma_start(out=dst[:], in_=src[:]), "c_l", 16)

        NCONST = kb.val("c_l")

        XCT = [at(f"XCT{i}", [128, D], BF16, offset=offD + 4 * KB_ + i * 2 * KB_)
               for i in range(4)]
        px = ExitStack()
        TRXB = [px.enter_context(nc.psum_tensor(f"trxb{i}", [128, 128], BF16))
                for i in range(2)]

        def emit_xc(tts):
            @blk.gpsimd
            def _(e):
                for tt in tts:
                    if tt >= 4:
                        e.wait_ge(kb.sem("xt_mm"), (tt - 3) * DC)
                    kb.inc(e.dma_start(out=XCT[tt % 4][:],
                                       in_=x_in[tt * 128:(tt + 1) * 128, :]),
                           f"xc_l{tt % 2}", 16)

        def emit_wcast(names):
            @blk.gpsimd
            def _(e):
                for i, wname in names:
                    for h in range(2):
                        kb.inc(e.dma_start(out=wbf_d[i][h * 512:(h + 1) * 512],
                                           in_=w_in[wname][h * 512:(h + 1) * 512]),
                               "wbf_w", 16)

        emit_xc(range(0, TT))
        emit_wcast([(0, "wk"), (1, "wq"), (2, "wv"), (3, "wp")])

        for i in range(4):
            def emit_wt(i=i):
                @blk.sync
                def _(e):
                    e.wait_ge(kb.sem("wbf_w"), 32 * (i + 1))
                    if i == 2:
                        e.wait_ge(kb.sem("kt_mm"), 32)
                    if i == 3:
                        e.wait_ge(kb.sem("qt_mm"), 32)
                    slot = WT0 if i % 2 == 0 else WT1
                    for ct in range(DC):
                        kb.inc(e.dma_start(out=slot[:, ct * D:(ct + 1) * D],
                                           in_=wbf_d[i][:, ct * 128:(ct + 1) * 128],
                                           transpose=True), f"wt_l{i % 2}", 16)
            emit_wt()

        @blk.tensor
        def _(e):
            e.wait_ge(kb.sem("c_l"), NCONST)
            n = 0
            for tt in range(TT):
                e.wait_ge(kb.sem(f"xc_l{tt % 2}"), 16 * (tt // 2 + 1))
                for ct in range(DC):
                    if n >= 2:
                        e.wait_ge(kb.sem("xt_ev"), n - 1)
                    kb.inc(e.transpose(TRXB[n % 2][:],
                                       XCT[tt % 4][:, ct * 128:(ct + 1) * 128],
                                       IDB[:]), "xt_mm")
                    n += 1

        @blk.vector
        def _(e):
            n = 0
            for tt in range(TT):
                for ct in range(DC):
                    e.wait_ge(kb.sem("xt_mm"), n + 1)
                    kb.inc(e.tensor_copy(XT[:, ct * T + tt * 128: ct * T + tt * 128 + 128],
                                         TRXB[n % 2][:]), "xt_ev")
                    n += 1


        p1 = ExitStack()
        with p1:
            TRP = p1.enter_context(nc.psum_tensor("trp", [128, 128], BF16))
            TRF = p1.enter_context(nc.psum_tensor("trf", [128, 128], F32))

            @blk.vector
            def _(e):
                kb.inc(e.memset(CONSTT[:, 0:1], EXP_BIAS), "cst_z")
                kb.inc(e.memset(CONSTT[:, 1:2], LN_EPS), "cst_z")
                e.wait_ge(kb.sem("c_l"), NCONST)
                e.drain()
                kb.inc(e.tensor_reduce(SRWB[0:E, 0:1], RWF[:], AX.X, ALU.add), "srw_r")
                e.drain()
                kb.inc(e.tensor_copy(RWB16[:], RWF[:]), "rwb_c")

            @blk.tensor
            def _(e):
                e.wait_ge(kb.sem("rwb_c"), 1)
                for ct in range(DC):
                    if ct >= 1:
                        e.wait_ge(kb.sem("rwt_ev"), ct)
                    kb.inc(e.transpose(TRP[:, 0:E], RWB16[:, ct * 128:(ct + 1) * 128],
                                       IDB[0:E, 0:E]), "rwt_mm")

            @blk.vector
            def _(e):
                for ct in range(DC):
                    e.wait_ge(kb.sem("rwt_mm"), ct + 1)
                    kb.inc(e.tensor_copy(RWT[:, ct * E:(ct + 1) * E], TRP[:, 0:E]), "rwt_ev")

            @blk.tensor
            def _(e):
                e.wait_ge(kb.sem("srw_r"), 1)
                kb.inc(e.transpose(TRF[0:1, 0:E], SRWB[0:E, 0:1], IDF[0:E, 0:E]), "srwt_mm")

            @blk.vector
            def _(e):
                e.wait_ge(kb.sem("srwt_mm"), 1)
                kb.inc(e.tensor_copy(SRWB[0:1, 0:E], TRF[0:1, 0:E]), "srwt_ev")

            @blk.tensor
            def _(e):
                e.wait_ge(kb.sem("srwt_ev"), 1)
                kb.inc(e.matmul(TRF[:, 0:E], ONESF[0:1, :], SRWB[0:1, 0:E],
                                start=True, stop=True), "srwb_mm")

            @blk.vector
            def _(e):
                e.wait_ge(kb.sem("srwb_mm"), 1)
                kb.inc(e.tensor_copy(SRWB[:, 0:E], TRF[:, 0:E]), "srwb_ev")

            @blk.gpsimd
            def _(e):
                e.wait_ge(kb.sem("wbf_w"), 128)
                e.wait_ge(kb.sem("xt_mm"), TT * DC)
                kb.inc(e.dma_start(out=WPN[:].rearrange("p (c f) -> p c f", f=D),
                                   in_=wbf_d[3].rearrange("(c p) d -> p c d", p=128)),
                       "wpn_l", 16)

            RWPP = p1.enter_context(nc.psum_tensor("rwpp", [E, 512], F32))

            @blk.tensor
            def _(e):
                e.wait_ge(kb.sem("wpn_l"), 16)
                e.wait_ge(kb.sem("rwt_ev"), DC)
                for oc in range(2):
                    if oc == 1:
                        e.wait_ge(kb.sem("rwp_ev"), 1)
                    for pc in range(DC):
                        mm = e.matmul(RWPP[:], RWT[:, pc * E:(pc + 1) * E],
                                      WPN[:, pc * D + oc * 512: pc * D + oc * 512 + 512],
                                      start=(pc == 0), stop=(pc == DC - 1))
                    kb.inc(mm, "rwp_mm")

            @blk.vector
            def _(e):
                for oc in range(2):
                    e.wait_ge(kb.sem("rwp_mm"), oc + 1)
                    kb.inc(e.tensor_copy(RWPE[:, oc * 512:(oc + 1) * 512], RWPP[:]),
                           "rwp_ev")

            @blk.tensor
            def _(e):
                e.wait_ge(kb.sem("rwp_ev"), 2)
                for ot in range(DC):
                    if ot >= 1:
                        e.wait_ge(kb.sem("rwpt_ev"), ot)
                    kb.inc(e.transpose(TRP[:, 0:E], RWPE[:, ot * 128:(ot + 1) * 128],
                                       IDB[0:E, 0:E]), "rwpt_mm")

            @blk.vector
            def _(e):
                for ot in range(DC):
                    e.wait_ge(kb.sem("rwpt_mm"), ot + 1)
                    kb.inc(e.tensor_copy(RWPT[:, ot * E:(ot + 1) * E], TRP[:, 0:E]),
                           "rwpt_ev")

        px.close()

        if stop_after < 1:
            return _epilog(nc, kb, blk, out_t, locals(), dbg_names, dbg_handles)

        p2 = ExitStack()
        with p2:
            PRJ = [p2.enter_context(nc.psum_tensor(f"prj{i}", [128, 512], F32))
                   for i in range(4)]

            def proj_omajor(tag, slot, dst, wt_sem, wt_val):
                @blk.tensor
                def _(e):
                    e.wait_ge(kb.sem(wt_sem), wt_val)
                    e.wait_ge(kb.sem("xt_ev"), TT * DC)
                    if tag == "kt":
                        e.wait_ge(kb.sem("rwpt_ev"), DC)
                    base = 0 if tag == "kt" else 2
                    n = 0
                    for ot in range(DC):
                        for tc in range(NQC):
                            ps = PRJ[base + n % 2]
                            if n >= 2:
                                e.wait_ge(kb.sem(f"{tag}_ev"), n - 1)
                            for ct in range(DC):
                                mm = e.matmul(
                                    ps[:], slot[:, ct * D + ot * 128: ct * D + ot * 128 + 128],
                                    XT[:, ct * T + tc * 512: ct * T + tc * 512 + 512],
                                    start=(ct == 0), stop=(ct == DC - 1))
                            kb.inc(mm, f"{tag}_mm")
                            n += 1

                @blk.vector
                def _(e):
                    base = 0 if tag == "kt" else 2
                    n = 0
                    for ot in range(DC):
                        for tc in range(NQC):
                            e.wait_ge(kb.sem(f"{tag}_mm"), n + 1)
                            kb.inc(e.tensor_scalar_mul(
                                dst[:, ot * T + tc * 512: ot * T + tc * 512 + 512],
                                PRJ[base + n % 2][:], 1.0 / 32.0), f"{tag}_ev")
                            n += 1

            proj_omajor("kt", WT0, KT, "wt_l0", 16 * DC)
            proj_omajor("qt", WT1, QT, "wt_l1", 16 * DC)

            @blk.vector
            def _(e):
                kb.inc(e.memset(V[:], 0.0), "v_init")
                e.drain()
                kb.inc(e.memset(
                    V[:].rearrange("p (t h u) -> p t h u", h=NH, u=U)[:, :, :, DH:U],
                    1.0), "v_init")

            @blk.tensor
            def _(e):
                e.wait_ge(kb.sem("wt_l0"), 16 * DC * 2)
                n = 0
                for tt in range(TT):
                    for oc in range(2):
                        ps = PRJ[n % 2]
                        if n < 2:
                            e.wait_ge(kb.sem("kt_ev"), 31 + n)
                        else:
                            e.wait_ge(kb.sem("v_ev"), n - 1)
                        for ct in range(DC):
                            mm = e.matmul(ps[:], XT[:, ct * T + tt * 128: ct * T + tt * 128 + 128],
                                          WT0[:, ct * D + oc * 512: ct * D + oc * 512 + 512],
                                          start=(ct == 0), stop=(ct == DC - 1))
                        kb.inc(mm, "v_mm")
                        n += 1

            @blk.vector
            def _(e):
                e.wait_ge(kb.sem("v_init"), 2)
                e.drain()
                n = 0
                for tt in range(TT):
                    for oc in range(2):
                        e.wait_ge(kb.sem("v_mm"), n + 1)
                        dstv = V[:].rearrange("p (t h u) -> p t h u", h=NH, u=U)[
                            :, tt, oc * 8:(oc + 1) * 8, 0:DH]
                        kb.inc(e.tensor_scalar_mul(
                            dstv, PRJ[n % 2][:].rearrange("p (h u) -> p h u", u=DH),
                            1.0 / 32.0), "v_ev")
                        n += 1

            XRWP = [p2.enter_context(nc.psum_tensor(f"xrw{i}", [128, E], F32))
                    for i in range(2)]

            @blk.tensor
            def _(e):
                e.wait_ge(kb.sem("rwt_ev"), DC)
                for tt in range(TT):
                    if tt >= 2:
                        e.wait_ge(kb.sem("xraw_ev"), tt - 1)
                    for ct in range(DC):
                        mm = e.matmul(XRWP[tt % 2][:],
                                      XT[:, ct * T + tt * 128: ct * T + tt * 128 + 128],
                                      RWT[:, ct * E:(ct + 1) * E],
                                      start=(ct == 0), stop=(ct == DC - 1))
                    kb.inc(mm, "xraw_mm")

            @blk.vector
            def _(e):
                for tt in range(TT):
                    e.wait_ge(kb.sem("xraw_mm"), tt + 1)
                    kb.inc(e.tensor_copy(xraw_sb[:, tt * E:(tt + 1) * E], XRWP[tt % 2][:]),
                           "xraw_ev")

        if stop_after < 2:
            return _epilog(nc, kb, blk, out_t, locals(), dbg_names, dbg_handles)

        rounds = [(h, qi) for h in range(NH) for qi in range(NQC)]
        blocks = []
        for r, (h, qi) in enumerate(rounds):
            for ki in range(4 * qi + 4):
                blocks.append((r, h, qi, ki, ki >= 4 * qi))
        NB = len(blocks)
        pmx_of = {}
        pmn = 0
        for n, (r, h, qi, ki, dg) in enumerate(blocks):
            if dg:
                pmn += 1
                pmx_of[n] = pmn
        def slot_of(n):
            return 2 * ((n // 2) % 4) + (n % 2)

        slot_prev_av = [None] * NB
        last_use = {}
        for n in range(NB):
            s = slot_of(n)
            slot_prev_av[n] = last_use.get(s)
            last_use[s] = n

        p3 = ExitStack()
        with p3:
            SCP2 = [p3.enter_context(nc.psum_tensor(f"scp{i}", [128, 1024], F32))
                    for i in range(2)]
            AVP = [p3.enter_context(nc.psum_tensor(f"avp{i}", [65, 512], F32))
                   for i in range(2)]
            BCP = p3.enter_context(nc.psum_tensor("bcp", [64, 512], F32))

            @blk.sync
            def _(e):
                e.wait_ge(kb.sem("rwp_mm"), 2)
                kb.inc(e.dma_start(out=CMASK[:].rearrange("p (c f) -> p c f", f=512),
                                   in_=cmask_in.rearrange("c p f -> p c f")), "cm_l", 16)

            @blk.tensor
            def _(e):
                e.wait_ge(kb.sem("qt_ev"), 32)
                e.wait_ge(kb.sem("kt_ev"), 32)
                e.wait_ge(kb.sem("v_ev"), 32)
                e.wait_ge(kb.sem("xraw_ev"), TT)
                e.wait_ge(kb.sem("rwpt_ev"), DC)
                NPAIR = NB // 2

                def emit_sc_pair(pr):
                    if pr >= 2:
                        e.wait_ge(kb.sem("pex"), pr - 1)

                    for n in (2 * pr, 2 * pr + 1):
                        r, h, qi, ki, dg = blocks[n]
                        ot, rb = h // 2, 64 * (h % 2)
                        kb.inc(e.matmul(
                            SCP2[pr % 2][:, (n % 2) * 512:(n % 2) * 512 + 512],
                            KT[rb:rb + 64, ot * T + ki * 128: ot * T + ki * 128 + 128],
                            QT[rb:rb + 64, ot * T + qi * 512: ot * T + qi * 512 + 512],
                            start=True, stop=True), "sc_mm")

                ended_rounds = []

                def emit_av_pair(pr):
                    for n in (2 * pr, 2 * pr + 1):
                        r, h, qi, ki, dg = blocks[n]
                        kmax = 4 * qi + 4
                        if ki == 0 and r >= 2:
                            e.wait_ge(kb.sem("ot_ev"), r - 1)
                        if dg:
                            e.wait_ge(kb.sem("pmx"), pmx_of[n])
                        else:
                            e.wait_ge(kb.sem("pex"), pr + 1)
                        kb.inc(e.matmul(
                            AVP[r % 2][:],
                            V[:, ki * NH * U + h * U: ki * NH * U + h * U + U],
                            PPROB[:, slot_of(n) * 512: slot_of(n) * 512 + 512],
                            start=(ki == 0), stop=(ki == kmax - 1)), "av_mm")
                        if ki == kmax - 1:
                            ended_rounds.append(r)

                def emit_bc(rounds_list):
                    for r in rounds_list:
                        e.wait_ge(kb.sem("rc_ev"), r + 1)
                        kb.inc(e.matmul(BCP[:], ONESF[64:65, 0:64], RCP[64:65, :],
                                        start=True, stop=True), "bc_mm")

                for p in range(4):
                    emit_sc_pair(p)
                pending_bc = []
                for k in range(2, NPAIR // 2):
                    emit_sc_pair(2 * k)
                    emit_sc_pair(2 * k + 1)
                    ended_rounds.clear()
                    emit_av_pair(2 * k - 4)
                    emit_av_pair(2 * k - 3)
                    emit_bc(pending_bc)
                    pending_bc = list(ended_rounds)
                for p in range(NPAIR - 4, NPAIR):
                    ended_rounds.clear()
                    emit_av_pair(p)
                    emit_bc(pending_bc)
                    pending_bc = list(ended_rounds)
                emit_bc(pending_bc)

            conv_chunks = []
            for ex in range(E):
                for c in range(8):
                    conv_chunks.append((w1bf_d[ex][c * 128:(c + 1) * 128],
                                        w1_in[ex][c * 128:(c + 1) * 128]))
                for c in range(8):
                    conv_chunks.append((w2bf_d[ex][c * 512:(c + 1) * 512],
                                        w2_in[ex][c * 512:(c + 1) * 512]))
            NCH = len(conv_chunks)

            @blk.gpsimd
            def _(e):
                e.wait_ge(kb.sem("cm_l"), 16)
                ci = 0
                for n, (r, h, qi, ki, dg) in enumerate(blocks):
                    if dg:
                        c = ki - 4 * qi
                        e.wait_ge(kb.sem("pex"), n // 2 + 1)
                        sl = PPROB[:, slot_of(n) * 512: slot_of(n) * 512 + 512]
                        kb.inc(e.tensor_mul(sl, sl, CMASK[:, c * 512:(c + 1) * 512]),
                               "pmx")
                    while ci < NCH and ci * (NB * 7 // 8) <= n * NCH:
                        o_, i_ = conv_chunks[ci]
                        kb.inc(e.dma_start(out=o_, in_=i_), "wbf_w", 16)
                        ci += 1
                for o_, i_ in conv_chunks[ci:]:
                    kb.inc(e.dma_start(out=o_, in_=i_), "wbf_w", 16)

            @blk.scalar
            def _(e):
                e.wait_ge(kb.sem("cst_z"), 2)
                for n0 in range(0, NB, 2):
                    pr = n0 // 2
                    r, h, qi, ki, dg = blocks[n0]
                    e.wait_ge(kb.sem("sc_mm"), n0 + 2)
                    prevs = [slot_prev_av[n0], slot_prev_av[n0 + 1]]
                    prevs = [p for p in prevs if p is not None]
                    if prevs:
                        e.wait_ge(kb.sem("av_mm"), max(prevs) + 1)
                    s0 = slot_of(n0)
                    kb.inc(e.activation(PPROB[:, s0 * 512: s0 * 512 + 1024],
                                        SCP2[pr % 2][:], AF.Exp,
                                        bias=CONSTT[:, 0:1], scale=EXP_SCALE), "pex")

            @blk.vector
            def _(e):
                avend = {}
                for n, (r, h, qi, ki, dg) in enumerate(blocks):
                    if ki == 4 * qi + 3:
                        avend[r] = n + 1
                for n, (r, h, qi, ki, dg) in enumerate(blocks):
                    ot, rb = h // 2, 64 * (h % 2)
                    if ki == 4 * qi + 3:
                        e.wait_ge(kb.sem("av_mm"), avend[r])
                        e.drain()
                        kb.inc(e.reciprocal(RCP[64:65, :], AVP[r % 2][64:65, :]), "rc_ev")
                        e.wait_ge(kb.sem("bc_mm"), r + 1)
                        e.drain()
                        kb.inc(e.tensor_copy(BCS[:], BCP[:]), None)
                        e.drain()
                        kb.inc(e.scalar_tensor_tensor(
                            OT[rb:rb + 64, ot * T + qi * 512: ot * T + qi * 512 + 512],
                            AVP[r % 2][0:64, :], 1.0, BCS[:], ALU.mult, ALU.mult),
                            "ot_ev")

        if stop_after < 3:
            return _epilog(nc, kb, blk, out_t, locals(), dbg_names, dbg_handles)

        p4 = ExitStack()
        with p4:
            FIN = [p4.enter_context(nc.psum_tensor(f"fin{i}", [128, 512], F32))
                   for i in range(2)]

            @blk.sync
            def _(e):
                e.wait_ge(kb.sem("sc_mm"), NB)
                for tt in range(TT):
                    if tt >= 2:
                        e.wait_ge(kb.sem("r1_ev"), (tt - 1) * 2)
                    kb.inc(e.dma_start(out=XRES[tt % 2][:],
                                       in_=x_in[tt * 128:(tt + 1) * 128, :]),
                           f"xres_l{tt % 2}", 16)

            @blk.tensor
            def _(e):
                e.wait_ge(kb.sem("ot_ev"), len(rounds))
                e.wait_ge(kb.sem("wt_l1"), 16 * DC * 2)
                n = 0
                for tt in range(TT):
                    for pc in range(2):
                        ps = FIN[n % 2]
                        if n >= 2:
                            e.wait_ge(kb.sem("r1_ev"), n - 1)
                        for oc in range(DC):
                            mm = e.matmul(ps[:], OT[:, oc * T + tt * 128: oc * T + tt * 128 + 128],
                                          WT1[:, oc * D + pc * 512: oc * D + pc * 512 + 512],
                                          start=(oc == 0), stop=(oc == DC - 1))
                        kb.inc(mm, "fin_mm")
                        n += 1

            @blk.vector
            def _(e):
                n = 0
                for tt in range(TT):
                    if tt >= 2:
                        e.wait_ge(kb.sem(f"r2_w{tt % 2}"), 16 * (tt // 2))
                        e.wait_ge(kb.sem("xln_c"), tt - 1)
                    for pc in range(2):
                        e.wait_ge(kb.sem("fin_mm"), n + 1)
                        e.wait_ge(kb.sem(f"xres_l{tt % 2}"), 16 * (tt // 2 + 1))
                        kb.inc(e.scalar_tensor_tensor(
                            R1T[tt % 2][:, pc * 512:(pc + 1) * 512], FIN[n % 2][:],
                            1.0 / 32.0, XRES[tt % 2][:, pc * 512:(pc + 1) * 512],
                            ALU.mult, ALU.add), "r1_ev")
                        n += 1
                    e.drain()
                    kb.inc(e.bn_stats(BN6[:, 0:6], R1T[tt % 2][:, 0:512]), None)
                    e.drain()
                    kb.inc(e.bn_stats(BN6[:, 6:12], R1T[tt % 2][:, 512:1024]), None)
                    e.drain()
                    kb.inc(e.bn_aggr(MV2[:], BN6[:]), "ln1_ba")
                    e.drain()
                    kb.inc(e.tensor_copy(MSTAT[:, tt:tt + 1], MV2[:, 0:1]), None)
                    e.wait_ge(kb.sem("ln1_sq"), tt + 1)
                    e.drain()
                    kb.inc(e.reciprocal(RSTAT[:, tt:tt + 1], SDV[:]), None)
                    e.drain()
                    kb.inc(e.tensor_scalar(
                        R1T[tt % 2][:], R1T[tt % 2][:], MSTAT[:, tt:tt + 1],
                        RSTAT[:, tt:tt + 1], ALU.subtract, ALU.mult), "ln1_ap")
                    e.drain()
                    kb.inc(e.tensor_copy(XLN[:, tt * D:(tt + 1) * D], R1T[tt % 2][:]),
                           "xln_c")

            @blk.scalar
            def _(e):
                for tt in range(TT):
                    e.wait_ge(kb.sem("ln1_ba"), tt + 1)
                    kb.inc(e.activation(SDV[:], MV2[:, 1:2], AF.Sqrt, bias=CONSTT[:, 1:2]),
                           "ln1_sq")

            @blk.gpsimd
            def _(e):
                for tt in range(TT):
                    e.wait_ge(kb.sem("ln1_ap"), tt + 1)
                    e.wait_ge(kb.sem("xln_c"), tt + 1)
                    kb.inc(e.dma_start(out=resid2_d[tt * 128:(tt + 1) * 128, :],
                                       in_=R1T[tt % 2][:]), f"r2_w{tt % 2}", 16)

        if stop_after < 4:
            return _epilog(nc, kb, blk, out_t, locals(), dbg_names, dbg_handles)

        p5 = ExitStack()
        with p5:
            ORW = [p5.enter_context(nc.psum_tensor(f"orw{i}", [128, E], F32))
                   for i in range(2)]

            @blk.tensor
            def _(e):
                e.wait_ge(kb.sem("r1_ev"), 2 * TT)
                for tt in range(TT):
                    if tt >= 2:
                        e.wait_ge(kb.sem("rsb_ev"), tt - 1)
                    for oc in range(DC):
                        mm = e.matmul(ORW[tt % 2][:],
                                      OT[:, oc * T + tt * 128: oc * T + tt * 128 + 128],
                                      RWPT[:, oc * E:(oc + 1) * E],
                                      start=(oc == 0), stop=(oc == DC - 1))
                    kb.inc(mm, "orw_mm")

            @blk.vector
            def _(e):
                e.wait_ge(kb.sem("ln1_ap"), TT)
                e.wait_ge(kb.sem("srwb_ev"), 1)
                for tt in range(TT):
                    e.wait_ge(kb.sem("orw_mm"), tt + 1)
                    kb.inc(e.tensor_scalar(MS8[:], SRWB[:, 0:E], MSTAT[:, tt:tt + 1],
                                           None, ALU.mult), None)
                    e.drain()
                    kb.inc(e.scalar_tensor_tensor(LG8[:], ORW[tt % 2][:], 1.0 / 32.0,
                                                  xraw_sb[:, tt * E:(tt + 1) * E],
                                                  ALU.mult, ALU.add), None)
                    e.drain()
                    kb.inc(e.tensor_sub(LG8[:], LG8[:], MS8[:]), None)
                    e.drain()
                    kb.inc(e.tensor_scalar(LG8[:], LG8[:], RSTAT[:, tt:tt + 1], None,
                                           ALU.mult), None)
                    e.drain()
                    kb.inc(e.reduce_max(NMX[:], LG8[:], AX.X, negate=True), "r_mx")
                    e.wait_ge(kb.sem("r_ex"), tt + 1)
                    e.drain()
                    kb.inc(e.reciprocal(SMR[:], SMS[:]), None)
                    e.drain()
                    kb.inc(e.tensor_scalar(R_sb[:, tt * E:(tt + 1) * E], LG8[:], SMR[:],
                                           None, ALU.mult), "rsb_ev")

            @blk.scalar
            def _(e):
                for tt in range(TT):
                    e.wait_ge(kb.sem("r_mx"), tt + 1)
                    kb.inc(e.activation(LG8[:], LG8[:], AF.Exp, bias=NMX[:],
                                        accum_out=SMS[:]), "r_ex")

        p5b = ExitStack()
        with p5b:
            RTP = [p5b.enter_context(nc.psum_tensor(f"rtp{i}", [E, 128], F32))
                   for i in range(2)]
            REPP = [p5b.enter_context(nc.psum_tensor(f"repp{i}", [128, 512], F32))
                    for i in range(2)]

            @blk.tensor
            def _(e):
                e.wait_ge(kb.sem("rsb_ev"), TT)
                for tt in range(TT):
                    if tt >= 2:
                        e.wait_ge(kb.sem("rt_ev"), tt - 1)
                    kb.inc(e.transpose(RTP[tt % 2][:],
                                       R_sb[:, tt * E:(tt + 1) * E], IDF[:]), "rt_mm")

            @blk.vector
            def _(e):
                for tt in range(TT):
                    e.wait_ge(kb.sem("rt_mm"), tt + 1)
                    kb.inc(e.tensor_copy(RT[:, tt * 128:(tt + 1) * 128], RTP[tt % 2][:]),
                           "rt_ev")

            @blk.tensor
            def _(e):
                e.wait_ge(kb.sem("rt_ev"), TT)
                for qc in range(4):
                    if qc >= 2:
                        e.wait_ge(kb.sem("rep_ev"), qc - 1)
                    kb.inc(e.matmul(REPP[qc % 2][:], BREP[:],
                                    RT[:, qc * 512:(qc + 1) * 512],
                                    start=True, stop=True), "rep_mm")

            @blk.vector
            def _(e):
                for qc in range(4):
                    e.wait_ge(kb.sem("rep_mm"), qc + 1)
                    kb.inc(e.tensor_copy(REPS[:, qc * 512:(qc + 1) * 512],
                                         REPP[qc % 2][:]), "rep_ev")

        if stop_after < 5:
            return _epilog(nc, kb, blk, out_t, locals(), dbg_names, dbg_handles)

        p6 = ExitStack()
        with p6:
            BIS = p6.enter_context(nc.psum_tensor("bis", [128, 2], F32))
            CN8 = p6.enter_context(nc.psum_tensor("cn8", [E, 1], F32))

            @blk.vector
            def _(e):
                e.wait_ge(kb.sem("rep_ev"), 4)
                kb.inc(e.memset(LOW8[:, 0:1], 0.0), "bi_init")
                kb.inc(e.memset(LOW8[:, 1:2], 1.0), "bi_init")

            for it in range(NBIS):
                @blk.tensor
                def _(e, it=it):
                    if it == 0:
                        e.wait_ge(kb.sem("bi_init"), 2)
                    else:
                        e.wait_ge(kb.sem("bi_up"), it * 2)
                    kb.inc(e.matmul(BIS[:], BREP[:], LOW8[:], start=True, stop=True),
                           "bi_rep")

                @blk.vector
                def _(e, it=it):
                    e.wait_ge(kb.sem("bi_rep"), it + 1)
                    kb.inc(e.tensor_copy(LWR[:], BIS[:]), None)
                    e.drain()
                    kb.inc(e.tensor_mul(TAU2[:], LWR[:, 1:2], JCOL[:]), None)
                    e.drain()
                    kb.inc(e.tensor_scalar(TAU2[:], TAU2[:], 1.0 / 17.0, None, ALU.mult),
                           None)
                    e.drain()
                    kb.inc(e.tensor_add(TAU[:], TAU2[:], LWR[:, 0:1]), None)
                    e.drain()
                    kb.inc(e.tensor_copy(TAUD[:, it:it + 1], TAU[:]), None)
                    e.drain()
                    kb.inc(e.tensor_scalar(BSCR[:], REPS[:], TAU[:], None, ALU.is_ge,
                                           op1=ALU.add, accum_out=CNT1[:]), None)
                    e.drain()
                    kb.inc(e.tensor_copy(CNTD[:, it:it + 1], CNT1[:]), None)
                    e.drain()
                    kb.inc(e.tensor_scalar(FLG1[:], CNT1[:], float(CAP) - 0.5, None,
                                           ALU.is_ge), "bi_fl")

                @blk.tensor
                def _(e, it=it):
                    e.wait_ge(kb.sem("bi_fl"), it + 1)
                    kb.inc(e.matmul(CN8[:], BSEG[:], FLG1[:], start=True, stop=True),
                           "bi_js")

                @blk.vector
                def _(e, it=it):
                    e.wait_ge(kb.sem("bi_js"), it + 1)
                    kb.inc(e.tensor_copy(JS8[:], CN8[:]), None)
                    e.drain()
                    kb.inc(e.tensor_copy(JSD[:, it:it + 1], JS8[:]), None)
                    e.drain()
                    kb.inc(e.tensor_mul(JW8[:], JS8[:], LOW8[0:E, 1:2]), None)
                    e.drain()
                    kb.inc(e.tensor_scalar(JW8[:], JW8[:], 1.0 / 17.0, None, ALU.mult),
                           None)
                    e.drain()
                    kb.inc(e.tensor_add(LOW8[0:E, 0:1], LOW8[0:E, 0:1], JW8[:]), "bi_up")
                    kb.inc(e.tensor_scalar(LOW8[0:E, 1:2], LOW8[0:E, 1:2], 1.0 / 17.0,
                                           None, ALU.mult), "bi_up")

        if stop_after < 6:
            return _epilog(nc, kb, blk, out_t, locals(), dbg_names, dbg_handles)

        p7 = ExitStack()
        with p7:
            TRX = [p7.enter_context(nc.psum_tensor(f"trx{i}", [128, E], F32))
                   for i in range(2)]
            B1P = [p7.enter_context(nc.psum_tensor(f"b1p{i}", [128, E], F32))
                   for i in range(2)]

            @blk.vector
            def _(e):
                e.wait_ge(kb.sem("bi_up"), 2 * NBIS)
                kb.inc(e.tensor_scalar(MSK[:], RT[:], LOW8[0:E, 0:1], None, ALU.is_ge),
                       None)
                e.drain()
                kb.inc(e.tensor_tensor_scan(CS[:], MSK[:], MSK[:], 0.0, ALU.add,
                                            ALU.bypass), None)
                e.drain()
                kb.inc(e.tensor_mul(CS[:], CS[:], MSK[:]), "sel_ct")

            @blk.tensor
            def _(e):
                e.wait_ge(kb.sem("sel_ct"), 1)
                for tt in range(TT):
                    if tt >= 2:
                        e.wait_ge(kb.sem("ctt_ev"), tt - 1)
                    kb.inc(e.transpose(TRX[tt % 2][:], CS[:, tt * 128:(tt + 1) * 128],
                                       IDF[0:E, 0:E]), "ctt_mm")

            @blk.vector
            def _(e):
                for tt in range(TT):
                    e.wait_ge(kb.sem("ctt_mm"), tt + 1)
                    if tt == TT - 1:
                        kb.inc(e.tensor_tensor(GME[:], MSK[:], RT[:], ALU.mult),
                               "sel_gme")
                        e.drain()
                    kb.inc(e.tensor_copy(CTT[:, tt * E:(tt + 1) * E], TRX[tt % 2][:]),
                           "ctt_ev")
                gbl = GHL[:].rearrange("p (c e u) -> p c e u", e=E, u=4)
                g3 = GMT[:].rearrange("p (t e) -> p t e", e=E)
                e.drain()
                kb.inc(e.tensor_scalar(GMT[:], CTT[:], 0.0, None, ALU.is_gt), None)
                e.drain()
                kb.inc(e.tensor_mul(GMT[:], GMT[:], R_sb[:]), None)
                e.drain()
                kb.inc(e.tensor_copy(gbl[:, :, :, 0:1],
                                     g3.rearrange("p t e -> p t e ()")), None)
                e.drain()
                kb.inc(e.scalar_tensor_tensor(
                    gbl[:, :, :, 1:2], gbl[:, :, :, 0:1], -1.0,
                    g3.rearrange("p t e -> p t e ()"), ALU.mult, ALU.add), None)
                e.drain()
                kb.inc(e.tensor_copy(
                    gbl[:, :, :, 2:3],
                    PCCOL[:, 0:1].rearrange("p (t x) -> p t x ()", t=1)
                    .to_broadcast([128, TT, E, 1])), None)
                e.drain()
                kb.inc(e.tensor_copy(
                    gbl[:, :, :, 3:4],
                    PCCOL[:, 2:2 + TT].rearrange("p t -> p t () ()")
                    .to_broadcast([128, TT, E, 1])), "gm_c")

            @blk.sync
            def _(e):
                e.wait_ge(kb.sem("sel_gme"), 1)
                e.wait_ge(kb.sem("ctt_mm"), TT)
                kb.inc(e.dma_start(out=B1F[:], in_=b1_in[:]), "b1_l", 16)
                kb.inc(e.dma_start(out=B2F[:], in_=b2_in[:]), "b2_l", 16)
                kb.inc(e.dma_start(out=IOTA[:], in_=iota_in[:]), "iota_l", 16)

            @blk.vector
            def _(e):
                e.wait_ge(kb.sem("b2_l"), 16)
                kb.inc(e.tensor_copy(B2B[:], B2F[:]), "b2b_c")
                kb.inc(e.memset(R4[:], 0.0), None)

            @blk.tensor
            def _(e):
                e.wait_ge(kb.sem("b1_l"), 16)
                for ht in range(HT):
                    if ht >= 2:
                        e.wait_ge(kb.sem("b1t_ev"), ht - 1)
                    kb.inc(e.transpose(B1P[ht % 2][:], B1F[:, ht * 128:(ht + 1) * 128],
                                       IDF[0:E, 0:E]), "b1t_mm")

            @blk.vector
            def _(e):
                for ht in range(HT):
                    e.wait_ge(kb.sem("b1t_mm"), ht + 1)
                    kb.inc(e.tensor_copy(B1T[:, ht * E:(ht + 1) * E], B1P[ht % 2][:]),
                           "b1t_ev")

            @blk.gpsimd
            def _(e):
                e.wait_ge(kb.sem("sel_gme"), 1)
                e.wait_ge(kb.sem("r2_w0"), 16 * (TT // 2))
                e.wait_ge(kb.sem("r2_w1"), 16 * (TT // 2))
                kb.inc(e.memset(HTB[:], 0.0), "htb_z")

        if stop_after < 7:
            return _epilog(nc, kb, blk, out_t, locals(), dbg_names, dbg_handles)

        p8 = ExitStack()
        with p8:
            XGP1 = p8.enter_context(nc.psum_tensor("xgp", [128, 1024], F32))
            XGP = [XGP1[:, 0:SH], XGP1[:, 512:512 + SH]]
            H2T = [p8.enter_context(nc.psum_tensor(f"h2p{i}", [128, 1024], F32))
                   for i in range(2)]
            H2P = [H2T[0][:, 0:SH], H2T[0][:, 512:512 + SH],
                   H2T[1][:, 0:SH], H2T[1][:, 512:512 + SH]]
            Y2P = [p8.enter_context(nc.psum_tensor(f"y2p{i}", [128, DWIN], F32))
                   for i in range(2)]
            TR4 = XGP1[:, 800:804]

            NHW = FF // HWIN
            NDW = D // DWIN

            for ex in range(E):
                @blk.vector
                def _(e, ex=ex):
                    if ex == 0:
                        e.wait_ge(kb.sem("iota_l"), 16)
                        e.wait_ge(kb.sem("ctt_ev"), TT)
                        e.wait_ge(kb.sem("rep_ev"), 4)
                    else:
                        e.wait_ge(kb.sem("g_mm"), kb.val("g_mm"))
                    for c in range(TT):
                        kb.inc(e.tensor_scalar(
                            PT[:, c * S:(c + 1) * S], IOTA[:],
                            CTT[:, c * E + ex: c * E + ex + 1], None,
                            ALU.is_equal), "pt_b")

                base_xgev = kb.val("xg_ev")
                base_m1mm_prev = kb.val("m1_mm")

                @blk.tensor
                def _(e, ex=ex, base_xgev=base_xgev):
                    e.wait_ge(kb.sem("pt_b"), kb.val("pt_b"))
                    if ex == 0:
                        e.wait_ge(kb.sem("xln_c"), TT)
                        e.wait_ge(kb.sem("gm_c"), 1)
                        e.wait_ge(kb.sem("htb_z"), 1)
                        e.wait_ge(kb.sem("b1t_ev"), HT)
                    e.wait_ge(kb.sem("gel_ev"), kb.val("gel_ev"))
                    e.wait_ge(kb.sem("sl_cp"), kb.val("sl_cp"))
                    GB = [XGP1, H2T[0], H2T[1]]
                    for dt in range(DC):
                        if dt >= 3:
                            e.wait_ge(kb.sem("xg_ev"), base_xgev + dt - 2)
                        gb = GB[dt % 3]
                        for c in range(TT):
                            m1 = e.matmul(gb[:, 0:SH],
                                          XLN[:, c * D + dt * 128: c * D + dt * 128 + 128],
                                          PT[:, c * S: c * S + SH],
                                          start=(c == 0), stop=(c == TT - 1))
                            m2 = e.matmul(gb[:, 512:512 + SH],
                                          XLN[:, c * D + dt * 128: c * D + dt * 128 + 128],
                                          PT[:, c * S + SH: c * S + 2 * SH],
                                          start=(c == 0), stop=(c == TT - 1))
                        kb.inc(m2, "g_mm")
                    e.wait_ge(kb.sem("xg_ev"), base_xgev + DC)
                    for c in range(TT):
                        m1 = e.matmul(XGP1[0:4, 0:SH],
                                      GHL[:, (c * E + ex) * 4:(c * E + ex) * 4 + 4],
                                      PT[:, c * S: c * S + SH],
                                      start=(c == 0), stop=(c == TT - 1))
                        m2 = e.matmul(XGP1[0:4, 512:512 + SH],
                                      GHL[:, (c * E + ex) * 4:(c * E + ex) * 4 + 4],
                                      PT[:, c * S + SH: c * S + 2 * SH],
                                      start=(c == 0), stop=(c == TT - 1))
                    kb.inc(m2, "g_mm")

                @blk.vector
                def _(e, ex=ex, base_xgev=base_xgev, base_m1mm_prev=base_m1mm_prev):
                    gbase = kb.val("g_mm") - (DC + 1)
                    e.wait_ge(kb.sem("m1_mm"), base_m1mm_prev)
                    GBv = [XGP1, H2T[0], H2T[1]]
                    for dt in range(DC):
                        e.wait_ge(kb.sem("g_mm"), gbase + dt + 1)
                        kb.inc(e.tensor_copy(
                            XGT[:, dt * S: dt * S + 2 * SH].rearrange(
                                "p (h x) -> p h x", x=SH),
                            GBv[dt % 3][:].rearrange(
                                "p (h x) -> p h x", x=512)[:, :, 0:SH]), "xg_ev")
                    e.wait_ge(kb.sem("g_mm"), gbase + DC + 1)
                    e.wait_ge(kb.sem("sl_mm"), kb.val("sl_mm"))
                    kb.inc(e.tensor_copy(
                        R4[:, 0:2 * SH].rearrange("p (h x) -> p h x", x=SH),
                        XGP1[0:4, :].rearrange(
                            "p (h x) -> p h x", x=512)[:, :, 0:SH]), "r4_ev")

                base_slcp = kb.val("sl_cp")

                @blk.tensor
                def _(e, ex=ex, base_slcp=base_slcp):
                    e.wait_ge(kb.sem("r4_ev"), kb.val("r4_ev"))
                    for st in range(NST):
                        if st >= 1:
                            e.wait_ge(kb.sem("sl_cp"), base_slcp + st)
                        kb.inc(e.transpose(TR4, R4[:, st * 128:(st + 1) * 128],
                                           IDF[0:4, 0:4]), "sl_mm")

                @blk.vector
                def _(e, ex=ex):
                    base = kb.val("sl_mm") - NST
                    if ex >= 2:
                        e.wait_ge(kb.sem("y_ev"), NDW * NST * (ex - 1))
                        e.wait_ge(kb.sem("sc_w"), 16 * NST * 2 * (ex - 1))
                    for st in range(NST):
                        e.wait_ge(kb.sem("sl_mm"), base + st + 1)
                        kb.inc(e.tensor_copy(SLS[:], TR4), "sl_cp")
                        e.drain()
                        kb.inc(e.tensor_add(GSB[ex % 2][:, st:st + 1],
                                            SLS[:, 0:1], SLS[:, 1:2]), "sl_gs")
                        kb.inc(e.scalar_tensor_tensor(FIDX[:], SLS[:, 3:4], 128.0,
                                                      SLS[:, 2:3], ALU.mult, ALU.add),
                               None)
                        e.drain()
                        kb.inc(e.tensor_copy(IDXCB[ex % 2][:, st:st + 1], FIDX[:]),
                               "sl_ix")

                @blk.sync
                def _(e, ex=ex):
                    if ex == 0:
                        e.wait_ge(kb.sem("wbf_w"), 128 + 16 * 128)
                    for hw in range(NHW):
                        w = ex * NHW + hw
                        if w >= 2:
                            e.wait_ge(kb.sem("m1_mm"), 2 * (w - 1))
                        kb.inc(e.dma_start(
                            out=W1S[w % 2][:].rearrange("p (c f) -> p c f", f=HWIN),
                            in_=w1bf_d[ex].rearrange("(c p) h -> p c h", p=128)[
                                :, :, hw * HWIN:(hw + 1) * HWIN]), f"w1_l{w % 2}", 16)

                base_gel = kb.val("gel_ev")

                @blk.tensor
                def _(e, ex=ex, base_gel=base_gel):
                    e.wait_ge(kb.sem("xg_ev"), kb.val("xg_ev"))
                    nht = 0
                    for hw in range(NHW):
                        w = ex * NHW + hw
                        e.wait_ge(kb.sem(f"w1_l{w % 2}"), 16 * (w // 2 + 1))
                        for hh in range(HWIN // 128):
                            a, b = H2P[2 * (nht % 2)], H2P[2 * (nht % 2) + 1]
                            if nht >= 2:
                                e.wait_ge(kb.sem("gel_ev"), base_gel + nht - 1)
                            for dc in range(DC):
                                lw = W1S[w % 2][:, dc * HWIN + hh * 128:
                                                dc * HWIN + hh * 128 + 128]
                                m1 = e.matmul(a, lw, XGT[:, dc * S: dc * S + SH],
                                              start=(dc == 0), stop=(dc == DC - 1))
                                m2 = e.matmul(b, lw, XGT[:, dc * S + SH: dc * S + 2 * SH],
                                              start=(dc == 0), stop=(dc == DC - 1))
                            kb.inc(m2, "m1_mm")
                            nht += 1

                base_m1 = kb.val("m1_mm") - HT
                base_m2mm_prev = kb.val("m2_mm")

                @blk.scalar
                def _(e, ex=ex, base_m1=base_m1, base_m2mm_prev=base_m2mm_prev):
                    e.wait_ge(kb.sem("m2_mm"), base_m2mm_prev)
                    for ht in range(HT):
                        e.wait_ge(kb.sem("m1_mm"), base_m1 + ht + 1)
                        bia = B1T[:, ht * E + ex: ht * E + ex + 1]
                        kb.inc(e.activation(
                            HTB[:, ht * SP: ht * SP + 2 * SH].rearrange(
                                "p (h x) -> p h x", x=SH),
                            H2T[ht % 2][:].rearrange(
                                "p (h x) -> p h x", x=512)[:, :, 0:SH],
                            AF.Gelu, bias=bia), "gel_ev")

                @blk.sync
                def _(e, ex=ex):
                    for dw in range(NDW):
                        u = ex * NDW + dw
                        if u >= 2:
                            e.wait_ge(kb.sem("m2_mm"), NST * (u - 1))
                        kb.inc(e.dma_start(
                            out=W2S[u % 2][:].rearrange("p (c f) -> p c f", f=DWIN),
                            in_=w2bf_d[ex].rearrange("(c p) d -> p c d", p=128)[
                                :, :, dw * DWIN:(dw + 1) * DWIN]), f"w2_l{u % 2}", 16)

                base_gel_end = kb.val("gel_ev")
                base_yev = kb.val("y_ev")

                @blk.tensor
                def _(e, ex=ex, base_gel_end=base_gel_end, base_yev=base_yev):
                    e.wait_ge(kb.sem("gel_ev"), base_gel_end)
                    nst = 0
                    for dw in range(NDW):
                        u = ex * NDW + dw
                        e.wait_ge(kb.sem(f"w2_l{u % 2}"), 16 * (u // 2 + 1))
                        for st in range(NST):
                            M = 128 if st < 4 else 32
                            ps = Y2P[nst % 2]
                            if nst >= 2:
                                e.wait_ge(kb.sem("y_ev"), base_yev + nst - 1)
                            for hc in range(HT):
                                mm = e.matmul(
                                    ps[0:M, :],
                                    HTB[:, hc * SP + st * 128: hc * SP + st * 128 + M],
                                    W2S[u % 2][:, hc * DWIN:(hc + 1) * DWIN],
                                    start=(hc == 0), stop=(hc == HT - 1))
                            kb.inc(mm, "m2_mm")
                            nst += 1
                        kb.inc(e.nop(), None)

                base_m2 = kb.val("m2_mm") - NDW * NST
                base_scw = kb.val("sc_w")

                @blk.vector
                def _(e, ex=ex, base_m2=base_m2, base_scw=base_scw):
                    e.wait_ge(kb.sem("sl_gs"), kb.val("sl_gs"))
                    n = 0
                    for dw in range(NDW):
                        dh, dq = dw // 2, dw % 2
                        for st in range(NST):
                            M = 128 if st < 4 else 32
                            e.wait_ge(kb.sem("m2_mm"), base_m2 + n + 1)
                            if dq == 0 and st == 0:
                                v = base_scw - 16 * NST * (1 - dh)
                                if v > 0:
                                    e.wait_ge(kb.sem("sc_w"), v)
                            kb.inc(e.tensor_scalar(
                                YBH[dh][0:M, st * 512 + dq * DWIN:
                                         st * 512 + (dq + 1) * DWIN],
                                Y2P[n % 2][0:M, :], GSB[ex % 2][0:M, st:st + 1], None,
                                ALU.mult), "y_ev")
                            n += 1

                base_yev2 = base_yev

                @blk.gpsimd
                def _(e, ex=ex, base_yev2=base_yev2):
                    e.wait_ge(kb.sem("sl_ix"), kb.val("sl_ix"))
                    if ex == 0:
                        e.wait_ge(kb.sem("r2_w0"), 16 * (TT // 2))
                        e.wait_ge(kb.sem("r2_w1"), 16 * (TT // 2))
                    for dh in range(2):
                        e.wait_ge(kb.sem("y_ev"), base_yev2 + (dh + 1) * 2 * NST)
                        for st in range(NST):
                            M = 128 if st < 4 else 32
                            e.wait_ge(kb.sem("sc_w"), kb.val("sc_w"))
                            kb.inc(e.indirect_dma_start(
                                out=resid2_d,
                                out_offset=IndirectOffsetOnAxis(
                                    IDXCB[ex % 2][0:M, st:st + 1], 0),
                                in_=YBH[dh][0:M, st * 512:(st + 1) * 512], in_offset=None,
                                element_offset=dh * 512,
                                compute_op=ALU.add), "sc_w", 16)

        if stop_after < 8:
            return _epilog(nc, kb, blk, out_t, locals(), dbg_names, dbg_handles)

        p9 = ExitStack()
        with p9:
            B2P = [p9.enter_context(nc.psum_tensor(f"b2p{i}", [128, 512], F32))
                   for i in range(2)]

            @blk.sync
            def _(e):
                e.wait_ge(kb.sem("sc_w"), kb.val("sc_w"))
                for tt in range(TT):
                    if tt >= 4:
                        e.wait_ge(kb.sem("o_ap"), tt - 3)
                    kb.inc(e.dma_start(out=R2T[tt % 4][:],
                                       in_=resid2_d[tt * 128:(tt + 1) * 128, :]),
                           f"r2_l{tt % 2}", 16)
                    if tt >= 1:
                        e.wait_ge(kb.sem("o_ap"), tt)
                        kb.inc(e.dma_start(out=out_t[(tt - 1) * 128:tt * 128, :],
                                           in_=OUTS[(tt - 1) % 4][:]),
                               f"out_w{(tt - 1) % 2}", 16)
                e.wait_ge(kb.sem("o_ap"), TT)
                kb.inc(e.dma_start(out=out_t[(TT - 1) * 128:TT * 128, :],
                                   in_=OUTS[(TT - 1) % 4][:]),
                       f"out_w{(TT - 1) % 2}", 16)
                e.wait_ge(kb.sem("out_w0"), kb.val("out_w0"))
                e.wait_ge(kb.sem("out_w1"), kb.val("out_w1"))

            @blk.tensor
            def _(e):
                e.wait_ge(kb.sem("sel_gme"), 1)
                e.wait_ge(kb.sem("b2b_c"), 1)
                e.wait_ge(kb.sem("y_ev"), kb.val("y_ev"))
                n = 0
                for tt in range(TT):
                    for pc in range(2):
                        if n >= 2:
                            e.wait_ge(kb.sem("b2t_ev"), n - 1)
                        kb.inc(e.matmul(B2P[n % 2][:], GME[:, tt * 128:(tt + 1) * 128],
                                        B2B[:, pc * 512:(pc + 1) * 512],
                                        start=True, stop=True), "b2t_mm")
                        n += 1

            @blk.vector
            def _(e):
                n = 0
                for tt in range(TT):
                    e.wait_ge(kb.sem(f"r2_l{tt % 2}"), 16 * (tt // 2 + 1))
                    for pc in range(2):
                        e.wait_ge(kb.sem("b2t_mm"), n + 1)
                        kb.inc(e.tensor_tensor(R2T[tt % 4][:, pc * 512:(pc + 1) * 512],
                                               R2T[tt % 4][:, pc * 512:(pc + 1) * 512],
                                               B2P[n % 2][:], ALU.add), "b2t_ev")
                        n += 1
                    e.drain()
                    kb.inc(e.bn_stats(BN6[:, 0:6], R2T[tt % 4][:, 0:512]), None)
                    e.drain()
                    kb.inc(e.bn_stats(BN6[:, 6:12], R2T[tt % 4][:, 512:1024]), None)
                    e.drain()
                    kb.inc(e.bn_aggr(MV2[:], BN6[:]), "ln2_ba")
                    e.wait_ge(kb.sem("ln2_sq"), tt + 1)
                    e.drain()
                    kb.inc(e.reciprocal(RSD2[:], SDV[:]), None)
                    e.drain()
                    if tt >= 4:
                        e.wait_ge(kb.sem(f"out_w{tt % 2}"), 16 * (tt // 2 - 1))
                    kb.inc(e.tensor_scalar(OUTS[tt % 4][:], R2T[tt % 4][:], MV2[:, 0:1],
                                           RSD2[:], ALU.subtract, ALU.mult), "o_ap")

            @blk.scalar
            def _(e):
                for tt in range(TT):
                    e.wait_ge(kb.sem("ln2_ba"), tt + 1)
                    kb.inc(e.activation(SDV[:], MV2[:, 1:2], AF.Sqrt, bias=CONSTT[:, 1:2]),
                           "ln2_sq")



        if dbg_names:
            return _epilog(nc, kb, blk, out_t, locals(), dbg_names, dbg_handles,
                           final=True)
    return nc, dbg_handles


def _epilog(nc, kb, blk, out_t, env, dbg_names, dbg_handles, final=False):
    snap = {k: v for k, v in kb.cnt.items() if v > 0}
    if dbg_names:
        @blk.sync
        def _(e):
            for k, v in snap.items():
                e.wait_ge(kb.sems[k], v)
            for name in dbg_names:
                t = env.get(name)
                assert t is not None, f"debug tensor {name} not found"
                shape = [int(s) for s in t.shape]
                h = nc.dram_tensor(f"dbg_{name}", shape, t.dtype,
                                   kind="ExternalOutput").ap()
                dbg_handles[name] = h
                kb.inc(e.dma_start(out=h, in_=t[:]), "dbg_w", 16)

    if not final:
        @blk.gpsimd
        def _(e):
            for k, v in snap.items():
                e.wait_ge(kb.sems[k], v)
            if dbg_names:
                e.wait_ge(kb.sem("dbg_w"), kb.val("dbg_w"))
            src = env["IDF"]
            kb.inc(e.dma_start(out=out_t[0:128, 0:128], in_=src[:]), "out_w", 16)
            e.wait_ge(kb.sem("out_w"), kb.val("out_w"))
    else:
        @blk.sync
        def _(e):
            e.wait_ge(kb.sem("dbg_w"), kb.val("dbg_w"))
    return nc, dbg_handles


def make_consts():
    c = {}
    c["id_bf"] = np.eye(128, dtype=ml_dtypes.bfloat16)
    c["id_f32"] = np.eye(128, dtype=np.float32)
    m = np.zeros((4, 128, 512), dtype=ml_dtypes.bfloat16)
    jj = np.arange(512)[None, :]
    pp = np.arange(128)[:, None]
    for cc in range(4):
        m[cc] = (jj >= pp + 128 * cc).astype(ml_dtypes.bfloat16)
    c["cmask"] = m
    c["iota_s"] = np.broadcast_to(np.arange(1, S + 1, dtype=np.float16), (128, S)).copy()
    c["jcol"] = ((np.arange(128) % 16) + 1).astype(np.float32).reshape(128, 1)
    c["brep"] = np.zeros((E, 128), np.float32)
    for ee in range(E):
        c["brep"][ee, ee * 16:(ee + 1) * 16] = 1.0
    c["bseg"] = np.ascontiguousarray(c["brep"].T)
    c["ones_f"] = np.ones((128, 128), np.float32)
    c["ones_b"] = np.ones((128, 128), ml_dtypes.bfloat16)
    pc = np.zeros((128, 2 + TT), dtype=ml_dtypes.bfloat16)
    pc[:, 0] = np.arange(128)
    for cc in range(TT):
        pc[:, 2 + cc] = cc
    c["pccol"] = pc
    return c


def make_inputs(L0_attn_x, L0_attn_W_key, L0_attn_W_query, L0_attn_W_value,
                L0_attn_W_proj, L0_ffn_router_w, L0_ffn_w1, L0_ffn_b1, L0_ffn_w2,
                L0_ffn_b2):
    consts = make_consts()
    shared = dict(
        wk=np.ascontiguousarray(L0_attn_W_key), wq=np.ascontiguousarray(L0_attn_W_query),
        wv=np.ascontiguousarray(L0_attn_W_value), wp=np.ascontiguousarray(L0_attn_W_proj),
        rw=np.ascontiguousarray(L0_ffn_router_w), w1=np.ascontiguousarray(L0_ffn_w1),
        b1=np.ascontiguousarray(L0_ffn_b1), w2=np.ascontiguousarray(L0_ffn_w2),
        b2=np.ascontiguousarray(L0_ffn_b2), **consts)
    xs = np.asarray(L0_attn_x)
    return [dict(x=np.ascontiguousarray(xs[c]), **shared) for c in range(xs.shape[0])]


def kernel(**inputs):
    nc, _ = build()
    in_maps = make_inputs(**inputs)
    res = run_bass_kernel_spmd(nc, in_maps, core_ids=list(range(8)))
    return np.stack([r["out"] for r in res.results], axis=0)
